# version 04
# baseline (speedup 1.0000x reference)
import os
import numpy as np
import ml_dtypes
from contextlib import ExitStack

import concourse.bass as bass
import concourse.mybir as mybir
from concourse.bass_utils import run_bass_kernel_spmd

F32 = mybir.dt.float32
F16 = mybir.dt.float16
BF16 = mybir.dt.bfloat16
AF = mybir.ActivationFunctionType
ALU = mybir.AluOpType
AX = mybir.AxisListType

S = 2048
NT = 16
NCH = 4
CH = 512
SCALE = float(1.0 / np.sqrt(128.0))
EPS = 1e-5
NEG = -1e30

ENGS = ("te", "ve", "se", "ge", "sy")


class Prog:
    def __init__(self, nc, sems):
        self.nc = nc
        self.ops = {e: [] for e in ENGS}
        self.sems = sems
        self.n = {k: 0 for k in sems}

    def op(self, eng, meth, *args, inc=None, **kw):
        pairs = ()
        r = None
        if inc is not None:
            self.n[inc] += 1
            r = self.n[inc]
            pairs = ((self.sems[inc], 1),)
        self.ops[eng].append((meth, args, kw, pairs))
        return r

    def dma(self, eng, out, in_, inc="WL", **kw):
        self.n[inc] += 16
        self.ops[eng].append(("dma_start", (), dict(out=out, in_=in_, **kw),
                              ((self.sems[inc], 16),)))
        return self.n[inc]

    def wait(self, eng, sem, val=None):
        if val is None:
            val = self.n[sem]
        self.ops[eng].append(("wait", (self.sems[sem], val), {}, ()))

    def vop(self, meth, *args, **kw):
        self.wait("ve", "VD")
        return self.op("ve", meth, *args, inc="VD", **kw)

    def sop(self, meth, *args, **kw):
        self.wait("se", "EX")
        return self.op("se", meth, *args, inc="EX", **kw)

    def barrier(self):
        own = {"te": "MM", "ve": "VD", "se": "EX", "ge": "GP"}
        for e in ENGS:
            if e in own:
                self.wait(e, own[e], self.n[own[e]])
        for e in ENGS:
            self.n["BAR"] += 1
            self.ops[e].append(("inc", (self.sems["BAR"], 1), {}, ()))
        tgt = self.n["BAR"]
        for e in ENGS:
            self.wait(e, "BAR", tgt)

    def replay(self, e, eng):
        for i, (meth, args, kw, inc) in enumerate(self.ops[eng]):
            try:
                if meth == "wait":
                    e.wait_ge(*args)
                elif meth == "inc":
                    e.sem_inc(*args)
                else:
                    inst = getattr(e, meth)(*args, **kw)
                    for s, k in inc:
                        inst.then_inc(s, k)
            except Exception:
                print(f"REPLAY FAIL {eng}[{i}] {meth} args={args} kw={kw}")
                raise



def _wT_heads(W, H):
    e, d = W.shape
    Wr = W.reshape(H, 128, d // 128, 128)
    return np.ascontiguousarray(Wr.transpose(0, 3, 2, 1)).reshape(H, 128, -1)

def _w_in_out(W):
    d, o = W.shape
    Wr = W.reshape(d // 128, 128, o // 128, 128)
    return np.ascontiguousarray(Wr.transpose(2, 1, 0, 3)).reshape(o // 128, 128, -1)

def _wo_dev(W):
    Wr = W.reshape(16, 128, 16, 128)
    return np.ascontiguousarray(Wr.transpose(0, 3, 2, 1)).reshape(16, 128, -1)

def _w2_dev(W):
    Wr = W.reshape(16, 128, 16, 128)
    return np.ascontiguousarray(Wr.transpose(2, 1, 0, 3)).reshape(16, 128, -1)

def _colmat(v):
    d, k = v.shape
    return np.ascontiguousarray(v.reshape(16, 128, k).transpose(1, 0, 2))


def host_prep(inp):
    g = {}
    f32 = np.float32
    A = lambda k: np.asarray(inp[k], f32)
    g["wq0"] = _wT_heads(A("L0_attn_Wq"), 16)
    g["wk0"] = _wT_heads(A("L0_attn_Wk"), 16)
    g["wv0"] = _wT_heads(A("L0_attn_Wv"), 16)
    g["wo0"] = _wo_dev(A("L0_attn_Wo"))
    W1, Wg, W2 = A("L0_ffn_W1"), A("L0_ffn_Wg"), A("L0_ffn_W2")
    g["w1s"] = np.ascontiguousarray(np.stack(
        [_w_in_out(A("L0_ffn_Ws1"))] + [_w_in_out(W1[e]) for e in range(7)]))
    g["wgs"] = np.ascontiguousarray(np.stack(
        [_w_in_out(A("L0_ffn_Wsg"))] + [_w_in_out(Wg[e]) for e in range(7)]))
    g["w2s"] = np.ascontiguousarray(np.stack(
        [_w2_dev(A("L0_ffn_Ws2"))] + [_w2_dev(W2[e]) for e in range(7)]))
    g["gw0"] = _colmat(A("L0_ffn_gate_w"))
    g["gb0"] = A("L0_ffn_gate_b").reshape(7, 1)
    g["bias0"] = A("L0_ffn_biases").reshape(1, 7)
    g["wq1"] = _wT_heads(A("L1_attn_wq"), 16)
    g["wk1"] = _wT_heads(A("L1_attn_wk"), 4)
    g["wv1"] = _wT_heads(A("L1_attn_wv"), 4)
    g["wo1"] = _wo_dev(A("L1_attn_wo"))
    g["wf1"] = _wT_heads(A("L1_ffn_W"), 16)
    g["bf1"] = np.ascontiguousarray(A("L1_ffn_b").reshape(16, 128).T)
    LA, LB = A("L1_ffn_lora_A"), A("L1_ffn_lora_B")
    g["lA"] = _colmat(np.ascontiguousarray(LA.transpose(2, 0, 1)).reshape(2048, 128))
    g["lB"] = np.ascontiguousarray(LB.transpose(0, 2, 1)).reshape(128, 2048)
    g["gw1"] = _colmat(np.ascontiguousarray(A("L1_ffn_gate_w").T))
    g["gb1"] = A("L1_ffn_gate_b").reshape(8, 1)
    inv = 10000.0 ** (-np.arange(64, dtype=np.float64) / 64.0)
    fr = np.outer(np.arange(S, dtype=np.float64), inv)
    c = np.cos(fr).T.astype(f32)
    sn = np.sin(fr).T.astype(f32)
    g["ctab"] = np.concatenate([c, c], 0).astype(np.float16)
    g["stab"] = np.concatenate([sn, sn], 0).astype(np.float16)
    R = np.zeros((128, 128), f32)
    for i in range(64):
        R[i, i + 64] = -1.0
        R[i + 64, i] = 1.0
    g["rmat"] = np.ascontiguousarray(R.T).astype(np.float16)
    g["ident_h"] = np.eye(128, dtype=np.float16)
    g["ident_f"] = np.eye(128, dtype=f32)
    ku = np.arange(128).reshape(-1, 1)
    u0 = np.arange(896).reshape(1, -1)
    g["mask0"] = (u0 >= ku + 384).astype(ml_dtypes.bfloat16)
    v1 = np.arange(1536).reshape(1, -1)
    g["mask1"] = ((v1 >= 512 + ku) & (v1 <= 1024 + ku)).astype(ml_dtypes.bfloat16)
    g["ones_h"] = np.ones((128, 128), np.float16)
    g["ones_b"] = np.ones((128, 128), ml_dtypes.bfloat16)
    g["ones_f"] = np.ones((128, 128), f32)
    rep = np.zeros((8, 128), f32)
    for e in range(8):
        rep[e, e * 16:(e + 1) * 16] = 1.0
    g["rep"] = rep.astype(np.float16)
    return g


def prep_x(x_b):
    return np.ascontiguousarray(np.asarray(x_b, np.float32).T.reshape(16, 128, S).transpose(1, 0, 2))


def unprep_out(dev):
    return np.ascontiguousarray(np.asarray(dev).transpose(1, 0, 2).reshape(2048, S).T)



def build_program(phases=("l0a", "l0m", "l1a", "l1f"), xin="x", xout="out", dbg=False):
    nc = bass.Bass("TRN2", target_bir_lowering=False)

    dram = {}
    def din(name, shape, dt):
        dram[name] = nc.dram_tensor("d_" + name, list(shape), dt, kind="ExternalInput").ap()
        return dram[name]

    dx = din("x", (128, 16, S), F32) if xin == "x" else None
    mstin = din("mstin", (128, 16, S), F16) if xin == "mst" else None
    dout = nc.dram_tensor("out", [128, 16, S], F32, kind="ExternalOutput").ap() \
        if xout == "out" else None
    mstout = nc.dram_tensor("mstout", [128, 16, S], F16, kind="ExternalOutput").ap() \
        if xout == "mst" else None

    din("wq0", (16, 128, 2048), F32); din("wk0", (16, 128, 2048), F32)
    din("wv0", (16, 128, 2048), F32); din("wo0", (16, 128, 2048), F32)
    din("w1s", (8, 16, 128, 2048), F32); din("wgs", (8, 16, 128, 2048), F32)
    din("w2s", (8, 16, 128, 2048), F32)
    din("gw0", (128, 16, 7), F32); din("gb0", (7, 1), F32); din("bias0", (1, 7), F32)
    din("wq1", (16, 128, 2048), F32); din("wk1", (4, 128, 2048), F32)
    din("wv1", (4, 128, 2048), F32); din("wo1", (16, 128, 2048), F32)
    din("wf1", (16, 128, 2048), F32); din("bf1", (128, 16), F32)
    din("lA", (128, 16, 128), F32); din("lB", (128, 2048), F32)
    din("gw1", (128, 16, 8), F32); din("gb1", (8, 1), F32)
    din("ctab", (128, S), F16); din("stab", (128, S), F16)
    din("rmat", (128, 128), F16); din("ident_h", (128, 128), F16)
    din("ident_f", (128, 128), F32)
    din("mask0", (128, 896), BF16); din("mask1", (128, 1536), BF16)
    din("ones_h", (128, 128), F16); din("ones_b", (128, 128), BF16)
    din("ones_f", (128, 128), F32); din("rep", (8, 128), F16)

    attn_scr = nc.dram_tensor("attn_scr", [16, 128, S], BF16).ap()
    bridge_scr = nc.dram_tensor("bridge_scr", [32, 128], F16).ap()
    wrow_scr = nc.dram_tensor("wrow_scr", [8, S], F16).ap()

    st = ExitStack()
    with st:
        sbA = lambda name, shape, dt: st.enter_context(nc.sbuf_tensor(name, list(shape), dt))[:]
        psA = lambda name, shape, dt: st.enter_context(nc.psum_tensor(name, list(shape), dt))[:]

        B1 = sbA("B1", (128, 16, S), F16)
        B2r = sbA("B2", (128, 30 * 1024), F16)
        SCR = sbA("SCR", (128, 16, S), F16)
        ctab = sbA("ctab", (128, S), F16); stab = sbA("stab", (128, S), F16)
        rmat = sbA("rmat", (128, 128), F16)
        id_h = sbA("id_h", (128, 128), F16); id_f = sbA("id_f", (128, 128), F32)
        ones_h = sbA("ones_h", (128, 128), F16); ones_b = sbA("ones_b", (128, 128), BF16)
        ones_f = sbA("ones_f", (128, 128), F32); repm = sbA("rep", (8, 128), F16)
        gw0 = sbA("gw0", (128, 16, 7), F16); gb0 = sbA("gb0", (7, 1), F32)
        bias0r = sbA("bias0r", (1, 7), F32); bias0b = sbA("bias0b", (128, 7), F32)
        gw1 = sbA("gw1", (128, 16, 8), F16); gb1 = sbA("gb1", (8, 1), F32)
        bf1 = sbA("bf1", (128, 16), F32)
        brcol = sbA("brcol", (128, 32), F32)
        br2 = sbA("br2", (32, 128), F16)
        mr_c = sbA("mr_c", (1, CH), F16)
        rr_c = sbA("rr_c", (1, CH), F16)
        wrow_c = sbA("wrow_c", (1, 1024), F16)
        w8_c = sbA("w8_c", (8, CH), F16)
        sc_a = sbA("sc_a", (1, 16), F32)
        biash = sbA("biash", (128, 1), F32)
        lr_sb = sbA("lr_sb", (1, CH), BF16)
        gm = sbA("gm", (128, 16), F32); gm2 = sbA("gm2", (128, 16), F32)
        stat_sb = sbA("stat_sb", (128, 32), F32)
        zsc = sbA("zsc", (128, 2, CH), F16)
        epsb = sbA("epsb", (128, 1), F32)

        pA = psA("pA", (128, 4, CH), F32)
        pB = psA("pB", (128, 4, CH), F32)
        pAh = pA.bitcast(F16)
        pBh = pB.bitcast(F16)

        semnames = ["WL", "XL", "ST", "MM", "EX", "VD", "GP", "BAR",
                    "WQ", "WKV", "AO0", "AO1", "WO0", "WO1", "ROW", "WR",
                    "MA", "MB", "SP", "OT0", "OT1", "OT2", "OT3"]
        sems = {k: st.enter_context(nc.semaphore(k)) for k in semnames}
        p = Prog(nc, sems)

        def carve(off, shape, dt=F16):
            n = int(np.prod(shape[1:]))
            scale = {F16: 1, BF16: 1}.get(dt, 2)
            v = B2r[:, off:off + n * scale]
            if dt != F16:
                v = v.bitcast(dt)
            if len(shape) > 2:
                mid = shape[1:-1]
                dims = " ".join(f"d{i}" for i in range(len(mid)))
                v = v.rearrange(f"p ({dims} n) -> p {dims} n",
                                **{f"d{i}": mid[i] for i in range(len(mid))})
            return v, off + n * scale

        for nm, dst, eng in [("ctab", ctab, "sy"), ("stab", stab, "sy"),
                             ("rmat", rmat, "sy"), ("ident_h", id_h, "sy"),
                             ("ident_f", id_f, "sy"), ("ones_h", ones_h, "sy"),
                             ("ones_b", ones_b, "sy"), ("ones_f", ones_f, "sy"),
                             ("rep", repm, "sy"), ("gb0", gb0, "sy"),
                             ("bias0", bias0r, "sy"), ("gb1", gb1, "sy"),
                             ("bf1", bf1, "sy")]:
            p.dma(eng, dst, dram[nm], "XL")
        p.dma("ge", gw0, dram["gw0"], "WL")
        p.dma("ge", gw1, dram["gw1"], "WL")
        if xin == "x":
            for t in range(4):
                p.dma("ge", B1[:, 4 * t:4 * (t + 1), :], dx[:, 4 * t:4 * (t + 1), :], "WL")
        else:
            for t in range(4):
                p.dma("sy", B1[:, 4 * t:4 * (t + 1), :], mstin[:, 4 * t:4 * (t + 1), :], "XL")
        for e in ENGS:
            p.wait(e, "XL")
            p.wait(e, "WL")
        p.op("ve", "memset", epsb, EPS)
        p.op("te", "matmul", pA[:, 0, 0:7], ones_f[0:1, :], bias0r, start=True, stop=True, inc="MM")
        p.wait("ve", "MM")
        p.op("ve", "tensor_copy", bias0b, pA[:, 0, 0:7], inc="VD")
        p.wait("te", "VD")

        def bridge(cols_ap, nelem, outs):
            k = cols_ap.shape[-1]
            p.wait("te", "VD")
            p.wait("te", "EX")
            p.op("te", "transpose", pB[0:k, 2, 0:128], cols_ap, id_f, inc="MM")
            p.wait("ve", "MM")
            p.op("ve", "tensor_copy", br2[0:k, :], pB[0:k, 2, 0:128], inc="VD")
            p.wait("sy", "VD")
            p.dma("sy", bridge_scr[0:k, :], br2[0:k, :], "ST")
            p.wait("sy", "ST")
            flat = bridge_scr.rearrange("a b -> (a b)").unsqueeze(0)
            tok = None
            for dst, offe in outs:
                n = dst.shape[-1]
                tok = p.dma("sy", dst, flat[:, offe:offe + n], "XL")
            return tok

        def join_ln(off0, final=False):
            if os.environ.get("JOIN_SKIP"):
                return
            z2, off = carve(0, (128, S))
            meanb, off = carve(off, (128, CH))
            rstdb, off = carve(off, (128, CH))
            tnorm, off = carve(off, (128, CH))
            mrow_v, off = carve(off, (128, S))
            rrow_v, off = carve(off, (128, S))
            mrow = mrow_v[0:1, :]
            rrow = rrow_v[0:1, :]
            if final:
                xst, off = carve(off, (128, 4, CH), F32)
            assert off <= 30 * 1024, off
            zt = lambda m: SCR[:, :, m * 128:(m + 1) * 128]
            sqtok = []
            p.wait("te", "VD")
            p.wait("te", "EX")
            for m in range(16):
                if m >= 1:
                    p.wait("ve", "MM", sqtok[m - 1])
                p.op("ve", "tensor_tensor",
                     z2.rearrange("p (t n) -> p t n", t=16), zt(m), zt(m),
                     op=ALU.mult, inc="VD")
                for dt_ in range(16):
                    p.op("te", "matmul", pB[:, 0, m:m + 1], SCR[:, dt_, m * 128:(m + 1) * 128],
                         ones_h[:, 0:1], start=(dt_ == 0), stop=(dt_ == 15))
                p.wait("te", "VD")
                for dt_ in range(16):
                    t = p.op("te", "matmul", pB[:, 1, m:m + 1],
                             z2[:, dt_ * 128:(dt_ + 1) * 128], ones_h[:, 0:1],
                             start=(dt_ == 0), stop=(dt_ == 15),
                             inc=("MM" if dt_ == 15 else None))
                sqtok.append(t)
            p.wait("ve", "MM")
            p.vop("tensor_copy", stat_sb[:, 0:16], pB[:, 0, 0:16])
            p.vop("tensor_copy", stat_sb[:, 16:32], pB[:, 1, 0:16])
            p.vop("tensor_single_scalar", brcol[:, 0:16], stat_sb[:, 0:16], 1.0 / 2048, ALU.mult)
            p.vop("tensor_mul", gm, brcol[:, 0:16], brcol[:, 0:16])
            p.vop("scalar_tensor_tensor", gm2, stat_sb[:, 16:32], 1.0 / 2048, gm,
                 op0=ALU.mult, op1=ALU.subtract)
            if dbg:
                p.wait("sy", "VD")
                p.dma("sy", dbg_var, gm2, "ST")
            p.wait("se", "VD")
            p.sop("activation", gm, gm2, AF.Ln, bias=epsb[:, 0:1], scale=1.0)
            p.sop("activation", brcol[:, 16:32], gm, AF.Exp, bias=0.0, scale=-0.5)
            p.wait("ve", "EX")
            bridge(brcol[:, 0:32], 32, [])
            sttok = []
            flat = bridge_scr.rearrange("a b -> (a b)").unsqueeze(0)
            p.dma("sy", mrow, flat[:, 0:2048], "ROW")
            rowt = p.dma("sy", rrow, flat[:, 2048:4096], "ROW")
            p.wait("te", "ROW", rowt)
            for ch in range(NCH):
                csl = slice(ch * CH, (ch + 1) * CH)
                p.wait("te", "VD")
                p.op("te", "matmul", pA[:, 0, :], ones_h[0:1, :], mrow[:, csl], start=True, stop=True)
                p.op("te", "matmul", pA[:, 1, :], ones_h[0:1, :], rrow[:, csl], start=True, stop=True, inc="MM")
                p.wait("ve", "MM")
                p.op("ve", "tensor_copy", meanb, pA[:, 0, :])
                p.op("ve", "tensor_copy", rstdb, pA[:, 1, :], inc="VD")
                for dt_ in range(16):
                    p.op("ve", "tensor_sub", tnorm, SCR[:, dt_, csl], meanb)
                    if final:
                        sl = dt_ % 4
                        osem = f"OT{sl}"
                        if ch * 16 + dt_ >= 4:
                            p.wait("ve", osem, sttok[ch * 16 + dt_ - 4])
                        p.op("ve", "tensor_mul", xst[:, sl, :], tnorm, rstdb, inc="VD")
                        p.wait("sy", "VD")
                        sttok.append(p.dma("sy", dout[:, dt_, csl], xst[:, sl, :], osem))
                    else:
                        p.op("ve", "tensor_mul", B1[:, dt_, csl], tnorm, rstdb,
                             inc=("VD" if dt_ == 15 else None))
            if final:
                for q_ in range(4):
                    p.wait("sy", f"OT{q_}")

        def attn_phase(wq_d, wk_d, wv_d, nH, nG, msel, wo_d):
            rep_ = nH // nG
            off = 0
            Pbuf, off = carve(off, (128, 16, CH), BF16)
            qt, off = carve(off, (128, S))
            kt, off = carve(off, (128, S))
            qraw, off = carve(off, (128, S))
            rot, off = carve(off, (128, S))
            kraw, off = carve(off, (128, S))
            vn, off = carve(off, (128, 16, 128), BF16)
            aout, off = carve(off, (128, S), BF16)
            wqs, off = carve(off, (128, 16, 128))
            wks, off = carve(off, (128, 16, 128))
            wvs, off = carve(off, (128, 16, 128))
            msk, off = carve(off, (128, 1536), BF16)
            recb, off = carve(off, (128, CH), BF16)
            lrow_s = lr_sb
            assert off <= 30 * 1024, off
            Pbuf2 = SCR.rearrange("p a n -> p (a n)").bitcast(BF16).rearrange("p (t n) -> p t n", t=64)[:, 0:16, :]
            Pring = [Pbuf, Pbuf2]
            mask = msk[:, 0:896] if msel == 0 else msk

            mld = p.dma("sy", msk[:, 0:896] if msel == 0 else msk,
                        dram["mask0"] if msel == 0 else dram["mask1"], "XL")
            p.wait("ge", "XL", mld)

            def allowed(j):
                if msel == 0:
                    return list(range(0, 4 * j + 4)), set(range(4 * j, 4 * j + 4))
                lo = max(0, 4 * j - 4)
                kts = list(range(lo, 4 * j + 4))
                return kts, set(kts)

            def mask_ap(j, kt_):
                dlt = 512 * j - 128 * kt_
                o = dlt + 384 if msel == 0 else dlt + 512
                return mask[:, o:o + 512]

            heads = [(h, h // rep_) for h in range(nH)]
            qtok = [p.dma("ge", wqs, wq_d[0], "WQ")]
            p.dma("ge", wks, wk_d[0], "WKV")
            p.dma("ge", wvs, wv_d[0], "WKV")
            kvtok = [p.n["WKV"]]

            def mm64(bank4, wsb, tok, sem):
                p.wait("te", sem, tok)
                for ch in range(NCH):
                    for dt_ in range(16):
                        p.op("te", "matmul", bank4[:, ch, :], wsb[:, dt_, :],
                             B1[:, dt_, ch * CH:(ch + 1) * CH],
                             start=(dt_ == 0), stop=(dt_ == 15),
                             inc=("MM" if (ch == 3 and dt_ == 15) else None))
                return p.n["MM"]

            def evac(dst, bank4):
                p.wait("se", "MM")
                return p.op("se", "copy", dst, bank4.rearrange("p a n -> p (a n)"), inc="EX")

            for hi, (h, g) in enumerate(heads):
                newg = hi == 0 or g != heads[hi - 1][1]
                p.wait("te", "EX")
                p.wait("te", "VD")
                mm64(pA, wqs, qtok[hi], "WQ")
                evac(qraw, pA)
                if newg:
                    mm64(pB, wks, kvtok[g], "WKV")
                    evac(kraw, pB)
                qraw_tok = p.n["EX"] - (1 if newg else 0)
                p.wait("te", "EX", qraw_tok)
                for ch in range(NCH):
                    p.op("te", "matmul", pA[:, ch, :], rmat, qraw[:, ch * CH:(ch + 1) * CH],
                         start=True, stop=True, inc=("MM" if ch == 3 else None))
                rq_tok = evac(rot, pA)
                if newg:
                    p.wait("te", "EX", rq_tok - 1)
                    for ch in range(NCH):
                        p.op("te", "matmul", pB[:, ch, :], rmat, kraw[:, ch * CH:(ch + 1) * CH],
                             start=True, stop=True, inc=("MM" if ch == 3 else None))
                p.wait("ve", "EX", rq_tok)
                p.op("ve", "tensor_mul", qraw, qraw, ctab)
                p.op("ve", "tensor_mul", rot, rot, stab)
                p.op("ve", "tensor_add", qt, qraw, rot, inc="VD")
                ropeq_tok = p.n["VD"]
                if newg:
                    p.wait("se", "VD", ropeq_tok)
                    rk_tok = evac(rot, pB)
                    p.wait("te", "EX", rq_tok)
                    mm64(pA, wvs, kvtok[g], "WKV")
                    p.wait("se", "VD", ropeq_tok)
                    vt_tok = evac(qraw, pA)
                    p.wait("ve", "EX", rk_tok)
                    p.op("ve", "tensor_mul", kraw, kraw, ctab)
                    p.op("ve", "tensor_mul", rot, rot, stab)
                    p.op("ve", "tensor_add", kt, kraw, rot, inc="VD")
                    p.wait("te", "EX", vt_tok)
                    for t4 in range(4):
                        for i4 in range(4):
                            kk = t4 * 4 + i4
                            p.op("te", "transpose", pBh[:, t4, i4 * 128:(i4 + 1) * 128],
                                 qraw[:, kk * 128:(kk + 1) * 128], id_h,
                                 inc=("MM" if i4 == 3 else None))
                        p.wait("ve", "MM")
                        p.op("ve", "tensor_copy",
                             vn[:, t4 * 4:(t4 + 1) * 4, :].rearrange("p a n -> p (a n)"),
                             pBh[:, t4, 0:512], inc="VD")
                mm_proj = p.n["MM"]
                if hi + 1 < nH:
                    p.wait("ge", "MM", mm_proj)
                    qtok.append(p.dma("ge", wqs, wq_d[hi + 1], "WQ"))
                    if heads[hi + 1][1] != g:
                        p.dma("ge", wks, wk_d[g + 1], "WKV")
                        p.dma("ge", wvs, wv_d[g + 1], "WKV")
                        kvtok.append(p.n["WKV"])
                p.wait("ve", "VD")
                p.wait("ve", "MM")
                p.op("ve", "tensor_mul", qraw, qt, qt)
                p.op("ve", "tensor_mul", rot, kt, kt, inc="VD")
                p.wait("te", "VD")
                for m in range(16):
                    p.op("te", "matmul", pB[:, 0, m:m + 1], qraw[:, m * 128:(m + 1) * 128],
                         ones_h[:, 0:1], start=True, stop=True)
                for m in range(16):
                    p.op("te", "matmul", pB[:, 1, m:m + 1], rot[:, m * 128:(m + 1) * 128],
                         ones_h[:, 0:1], start=True, stop=True,
                         inc=("MM" if m == 15 else None))
                p.wait("ve", "MM")
                p.vop("reduce_max", brcol[:, 0:1], pB[:, 0, 0:16], axis=AX.X)
                p.vop("reduce_max", brcol[:, 1:2], pB[:, 1, 0:16], axis=AX.X)
                p.vop("reduce_max", brcol[:, 2:3], brcol[:, 0:2], axis=AX.X)
                p.wait("te", "VD")
                p.op("te", "transpose", pB[0:1, 2, 0:128], brcol[:, 2:3], id_f, inc="MM")
                p.wait("ve", "MM")
                p.vop("reduce_max", sc_a[:, 0:1], pB[0:1, 2, 0:128], axis=AX.X)
                p.vop("tensor_single_scalar", sc_a[:, 1:2], sc_a[:, 0:1],
                      -1.02 * SCALE, ALU.mult)
                p.wait("te", "VD")
                p.op("te", "matmul", pB[:, 3, 0:1], ones_f[0:1, :], sc_a[:, 1:2],
                     start=True, stop=True, inc="MM")
                p.wait("ve", "MM")
                p.wait("ve", "EX")
                p.op("ve", "tensor_copy", biash, pB[:, 3, 0:1], inc="VD")
                bias_vd = p.n["VD"]
                p.wait("se", "VD")

                flash_ex0 = p.n["EX"]
                extok = {}
                gptok = {}
                sbank_ex = {}
                pb_first = [True]
                amtok = [bias_vd]

                def emit_S(j):
                    kts, partial = allowed(j)
                    qsl = slice(j * CH, (j + 1) * CH)
                    P = Pring[j % 2]
                    nb = (len(kts) + 2) // 3
                    for b in range(nb):
                        bank = pA if b % 2 == 0 else pB
                        sub = kts[b * 3:(b + 1) * 3]
                        p.wait("te", "EX", sbank_ex.get(b % 2, flash_ex0))
                        if b % 2 == 1 and pb_first[0]:
                            p.wait("te", "VD", bias_vd)
                            pb_first[0] = False
                        for i, kt_ in enumerate(sub):
                            p.op("te", "matmul", bank[:, i, :], kt[:, kt_ * 128:(kt_ + 1) * 128],
                                 qt[:, qsl], start=True, stop=True,
                                 inc=("MM" if i == len(sub) - 1 else None))
                        p.wait("se", "MM")
                        for i, kt_ in enumerate(sub):
                            p.op("se", "activation", P[:, kt_ % 16, :], bank[:, i, :],
                                 AF.Exp, bias=biash[:, 0:1], scale=SCALE,
                                 inc=("EX" if i == len(sub) - 1 else None))
                        ex_now = p.n["EX"]
                        sbank_ex[b % 2] = ex_now
                        for i, kt_ in enumerate(sub):
                            if kt_ in partial:
                                p.wait("ve", "EX", ex_now)
                                p.op("ve", "tensor_mul", P[:, kt_ % 16, :],
                                     P[:, kt_ % 16, :], mask_ap(j, kt_), inc="VD")
                    extok[j] = p.n["EX"]
                    gptok[j] = p.n["VD"]

                def emit_tail(j):
                    kts, _ = allowed(j)
                    qsl = slice(j * CH, (j + 1) * CH)
                    P = Pring[j % 2]
                    p.wait("te", "EX", extok[j])
                    p.wait("te", "VD", max(gptok[j], amtok[0]))
                    for i, kt_ in enumerate(kts):
                        p.op("te", "matmul", pB[0:1, 3, :], ones_b[:, 0:1], P[:, kt_ % 16, :],
                             start=(i == 0), stop=(i == len(kts) - 1),
                             inc=("MM" if i == len(kts) - 1 else None))
                    mm_l = p.n["MM"]
                    for i, kt_ in enumerate(kts):
                        p.op("te", "matmul", pA[:, 3, :], vn[:, kt_, :], P[:, kt_ % 16, :],
                             start=(i == 0), stop=(i == len(kts) - 1),
                             inc=("MM" if i == len(kts) - 1 else None))
                    p.wait("ve", "MM", mm_l)
                    p.op("ve", "tensor_copy", lrow_s, pB[0:1, 3, :], inc="VD")
                    p.wait("te", "VD")
                    p.op("te", "matmul", pB[:, 3, :], ones_b[0:1, :], lrow_s,
                         start=True, stop=True, inc="MM")
                    p.wait("se", "MM")
                    p.op("se", "copy", aout[:, qsl], pA[:, 3, :], inc="EX")
                    p.wait("ve", "EX")
                    p.wait("ve", "MM")
                    p.op("ve", "reciprocal", recb, pB[:, 3, :])
                    p.op("ve", "tensor_mul", aout[:, qsl], aout[:, qsl], recb, inc="VD")
                    amtok[0] = p.n["VD"]

                emit_S(0)
                for j in range(1, NCH):
                    emit_S(j)
                    emit_tail(j - 1)
                emit_tail(NCH - 1)
                p.wait("sy", "VD")
                p.dma("sy", attn_scr[h], aout, "SP")

            p.barrier()
            off = 0
            ao, off = carve(off, (128, 2, 16, CH), BF16)
            wos, off = carve(off, (128, 2, 16, 128), BF16)
            assert off <= 30 * 1024
            a_t = attn_scr.rearrange("h p n -> p h n")
            p.wait("sy", "SP")
            p.wait("ge", "SP")
            aotok = [p.dma("sy", ao[:, 0], a_t[:, :, 0:CH], "AO0")]
            wotok = [p.dma("ge", wos[:, 0], wo_d[0], "WO0")]
            mmtok = []
            for ch in range(NCH):
                if ch + 1 < NCH:
                    if ch >= 1:
                        p.wait("sy", "MM", mmtok[ch * 16 - 1])
                    aotok.append(p.dma("sy", ao[:, (ch + 1) % 2],
                                       a_t[:, :, (ch + 1) * CH:(ch + 2) * CH],
                                       "AO1" if (ch + 1) % 2 else "AO0"))
                for dt_ in range(16):
                    widx = ch * 16 + dt_
                    if widx + 1 < 64:
                        if widx >= 1:
                            p.wait("ge", "MM", mmtok[widx - 1])
                        wotok.append(p.dma("ge", wos[:, (widx + 1) % 2],
                                           wo_d[(widx + 1) % 16],
                                           "WO1" if (widx + 1) % 2 else "WO0"))
                    p.wait("te", "AO1" if ch % 2 else "AO0", aotok[ch])
                    p.wait("te", "WO1" if widx % 2 else "WO0", wotok[widx])
                    p.wait("te", "VD")
                    bank = pA[:, widx % 2, :]
                    for hh in range(16):
                        p.op("te", "matmul", bank, wos[:, widx % 2, hh, :],
                             ao[:, ch % 2, hh, :], start=(hh == 0), stop=(hh == 15),
                             inc=("MM" if hh == 15 else None))
                    mmtok.append(p.n["MM"])
                    p.wait("se", "MM")
                    p.wait("se", "VD")
                    p.op("se", "copy", zsc[:, widx % 2, :], bank, inc="EX")
                    p.wait("ve", "EX")
                    p.op("ve", "tensor_add", SCR[:, dt_, ch * CH:(ch + 1) * CH],
                         zsc[:, widx % 2, :], B1[:, dt_, ch * CH:(ch + 1) * CH], inc="VD")
                    if dbg and ch == 0 and dt_ == 15:
                        p.dma("sy", dbg_ao, ao[:, 0], "ST")

        def moe_phase():
            off0 = 0
            actp, off0 = carve(off0, (128, 16, 1024), BF16)
            wstg, off0 = carve(off0, (128, 2, 2, 16, 128))
            w2st = wstg.bitcast(BF16)
            wb, off0 = carve(off0, (128, 2, CH), BF16)
            sil, off0 = carve(off0, (128, CH), BF16)
            sil2, off0 = carve(off0, (128, CH), BF16)
            joff = off0
            go = 0
            glr, go = carve(go, (128, 2048), F32)
            gtok, go = carve(go, (128, 128), F32)
            gtk2, go = carve(go, (128, 128), F32)
            gt_e1, go = carve(go, (128, 128), F32)
            gt_e2, go = carve(go, (128, 128), F32)
            gt_pw, go = carve(go, (128, 128), F32)
            w7sb_, go = carve(go, (128, S))
            w7sb = w7sb_[0:8, :]
            assert go <= 16 * 1024

            class _RouteStop(Exception):
                pass
            try:
              if not os.environ.get("MOE_SKIP_ROUTE"):
                for ch in range(NCH):
                    for dt_ in range(16):
                        p.op("te", "matmul", pA[0:7, ch, :], gw0[:, dt_, :],
                             B1[:, dt_, ch * CH:(ch + 1) * CH],
                             start=(dt_ == 0), stop=(dt_ == 15),
                             inc=("MM" if (ch == 3 and dt_ == 15) else None))
                glv = glr[0:7, :].rearrange("p (a n) -> p a n", a=4)
                p.wait("ve", "MM")
                for ch in range(NCH):
                    p.op("ve", "tensor_single_scalar", glv[:, ch, :], pA[0:7, ch, :],
                         gb0[:, 0:1], ALU.add, inc=("VD" if ch == 3 else None))
                p.wait("te", "VD")
                for m in range(16):
                    p.op("te", "transpose", pB[:, 2, m * 8:m * 8 + 7],
                         glv[:, m // 4, (m % 4) * 128:(m % 4 + 1) * 128], id_f[0:7, 0:7],
                         inc=("MM" if m == 15 else None))
                p.wait("ve", "MM")
                p.vop("memset", gtok, NEG)
                p.vop("tensor_copy",
                     gtok.rearrange("p (a n) -> p a n", a=16)[:, :, 0:7],
                     pB[:, 2, 0:128].rearrange("p (a n) -> p a n", a=16)[:, :, 0:7])
                _rp = int(os.environ.get("MOE_ROUTE_PART", "3"))
                if _rp < 2:
                    raise _RouteStop
                p.wait("se", "VD")
                p.op("se", "activation", gtk2, gtok, AF.Exp, bias=0.0, scale=1.0, inc="EX")
                p.wait("ve", "EX")
                gv = lambda t, m: t[:, m * 8:(m + 1) * 8]
                for m in range(16):
                    p.vop("reduce_sum", gm[:, m:m + 1], gv(gtk2, m), axis=AX.X)
                p.vop("reciprocal", gm2, gm)
                for m in range(16):
                    p.vop("tensor_single_scalar", gv(gtk2, m), gv(gtk2, m),
                          gm2[:, m:m + 1], ALU.mult)
                    p.vop("memset", gv(gt_pw, m), NEG)
                    p.vop("tensor_add", gv(gt_pw, m)[:, 0:7], gv(gtk2, m)[:, 0:7], bias0b[:, 0:7])
                    p.vop("reduce_max", gm[:, m:m + 1], gv(gt_pw, m), axis=AX.X)
                    p.vop("tensor_single_scalar", gv(gt_e1, m), gv(gt_pw, m),
                          gm[:, m:m + 1], ALU.is_equal)
                    p.vop("scalar_tensor_tensor", gv(gt_pw, m), gv(gt_e1, m), NEG,
                          gv(gt_pw, m), op0=ALU.mult, op1=ALU.add)
                    p.vop("reduce_max", gm[:, m:m + 1], gv(gt_pw, m), axis=AX.X)
                    p.vop("tensor_single_scalar", gv(gt_e2, m), gv(gt_pw, m),
                          gm[:, m:m + 1], ALU.is_equal)
                    p.vop("scalar_tensor_tensor", gv(gt_pw, m), gv(gt_e2, m), NEG,
                          gv(gt_pw, m), op0=ALU.mult, op1=ALU.add)
                    p.vop("reduce_max", gm[:, m:m + 1], gv(gt_pw, m), axis=AX.X)
                    p.vop("tensor_single_scalar", gv(gt_pw, m), gv(gt_pw, m),
                          gm[:, m:m + 1], ALU.is_equal)
                    p.vop("tensor_add", gv(gt_e1, m), gv(gt_e1, m), gv(gt_e2, m))
                    p.vop("tensor_add", gv(gt_e1, m), gv(gt_e1, m), gv(gt_pw, m))
                    p.vop("tensor_mul", gv(gt_pw, m), gv(gtk2, m), gv(gt_e1, m))
                    p.vop("reduce_sum", gm[:, m:m + 1], gv(gt_pw, m), axis=AX.X)
                    p.vop("reciprocal", gm2[:, m:m + 1], gm[:, m:m + 1])
                    p.vop("tensor_single_scalar", gv(gtok, m), gv(gt_pw, m),
                          gm2[:, m:m + 1], ALU.mult)
                if _rp < 3:
                    raise _RouteStop
                p.wait("te", "VD")
                for m in range(16):
                    p.op("te", "transpose", pA[0:8, m // 4, (m % 4) * 128:(m % 4 + 1) * 128],
                         gv(gtok, m), id_f, inc=("MM" if m == 15 else None))
                p.wait("ve", "MM")
                p.op("ve", "tensor_copy", w7sb,
                     pA[0:8, :, :].rearrange("p a n -> p (a n)"), inc="VD")
                p.wait("sy", "VD")
                p.dma("sy", wrow_scr, w7sb, "ST")
                p.wait("sy", "ST")
            except _RouteStop:
                pass

            if os.environ.get("MOE_STOP") == "route":
                return joff
            _ne = int(os.environ.get("MOE_NE", "8"))
            _nh = int(os.environ.get("MOE_NH", "2"))
            wrtok = None
            last_w2 = None
            vd_route = p.n["VD"]
            banktok = {}
            for half in range(_nh):
                hb = half * 1024
                for e in range(_ne):
                    if e > 0:
                        if wrtok is not None:
                            p.wait("sy", "MM", wrtok)
                        wt = p.dma("sy", wrow_c, wrow_scr[e - 1:e, hb:hb + 1024], "WR")
                        p.wait("te", "WR", wt)
                        for c2 in range(2):
                            p.wait("te", "VD", banktok.get((1, 1), vd_route))
                            p.op("te", "matmul", pB[:, 3, :], ones_h[0:1, :],
                                 wrow_c[:, c2 * CH:(c2 + 1) * CH], start=True, stop=True,
                                 inc="MM")
                            p.wait("ve", "MM")
                            wrtok = p.n["MM"]
                            p.op("ve", "tensor_copy", wb[:, c2, :], pB[:, 3, :], inc="VD")
                            banktok[(1, 1)] = p.n["VD"]
                    if last_w2 is not None:
                        p.wait("ge", "MM", last_w2)
                    p.dma("ge", wstg[:, 0, 0], dram["w1s"][e, 0], "MA")
                    p.dma("ge", wstg[:, 0, 1], dram["wgs"][e, 0], "MA")
                    wgt = [p.n["MA"]]
                    httok = []
                    _nht = int(os.environ.get("MOE_NHT", "16"))
                    for Ht in range(_nht):
                        slot = Ht % 2
                        ssem = "MB" if (Ht + 1) % 2 else "MA"
                        if Ht + 1 < _nht:
                            if Ht >= 1:
                                p.wait("ge", "MM", httok[Ht - 1])
                            p.dma("ge", wstg[:, (Ht + 1) % 2, 0],
                                  dram["w1s"][e, Ht + 1], ssem)
                            p.dma("ge", wstg[:, (Ht + 1) % 2, 1],
                                  dram["wgs"][e, Ht + 1], ssem)
                            wgt.append(p.n[ssem])
                        p.wait("te", "MB" if Ht % 2 else "MA", wgt[Ht])
                        for c2 in range(2):
                            csl = slice(hb + c2 * CH, hb + (c2 + 1) * CH)
                            bk = pA if Ht % 2 == 0 else pB
                            p.wait("te", "VD", banktok.get((Ht % 2, c2), vd_route))
                            for dt_ in range(16):
                                p.op("te", "matmul", bk[:, 2 + c2, :], wstg[:, slot, 1, dt_, :],
                                     B1[:, dt_, csl], start=(dt_ == 0), stop=(dt_ == 15))
                            for dt_ in range(16):
                                p.op("te", "matmul", bk[:, c2, :], wstg[:, slot, 0, dt_, :],
                                     B1[:, dt_, csl], start=(dt_ == 0), stop=(dt_ == 15),
                                     inc=("MM" if dt_ == 15 else None))
                            p.wait("se", "MM")
                            p.op("se", "activation", sil, bk[:, c2, :], AF.Silu,
                                 bias=0.0, scale=1.0, inc="EX")
                            p.wait("ve", "EX")
                            p.op("ve", "tensor_tensor", sil2, sil, bk[:, 2 + c2, :], op=ALU.mult)
                            if e == 0:
                                p.op("ve", "tensor_copy", actp[:, Ht, c2 * CH:(c2 + 1) * CH],
                                     sil2, inc="VD")
                            else:
                                p.op("ve", "tensor_mul", actp[:, Ht, c2 * CH:(c2 + 1) * CH],
                                     sil2, wb[:, c2, :], inc="VD")
                            banktok[(Ht % 2, c2)] = p.n["VD"]
                        httok.append(p.n["MM"])
                    if os.environ.get("MOE_STOP") == "h1":
                        continue
                    p.wait("ge", "MM", httok[14])
                    w2t = [p.dma("ge", w2st[:, 0, 0], dram["w2s"][e, 0], "MA")]
                    dttok = []
                    vd_act = p.n["VD"]
                    for dt_ in range(16):
                        slot = dt_ % 2
                        ssem = "MB" if (dt_ + 1) % 2 else "MA"
                        if dt_ + 1 < 16:
                            if dt_ >= 1:
                                p.wait("ge", "MM", dttok[dt_ - 1])
                            else:
                                p.wait("ge", "MM", httok[15])
                            w2t.append(p.dma("ge", w2st[:, (dt_ + 1) % 2, 0],
                                             dram["w2s"][e, dt_ + 1], ssem))
                        p.wait("te", "MB" if dt_ % 2 else "MA", w2t[dt_])
                        if dt_ == 0:
                            p.wait("te", "VD", vd_act)
                        for c2 in range(2):
                            csl = slice(hb + c2 * CH, hb + (c2 + 1) * CH)
                            bk = (pA if dt_ % 2 == 0 else pB)[:, c2, :]
                            p.wait("te", "VD", banktok.get((dt_ % 2, c2), vd_route))
                            for Ht in range(16):
                                p.op("te", "matmul", bk, w2st[:, slot, 0, Ht, :],
                                     actp[:, Ht, c2 * CH:(c2 + 1) * CH],
                                     start=(Ht == 0), stop=(Ht == 15),
                                     inc=("MM" if Ht == 15 else None))
                            p.wait("se", "MM")
                            p.wait("se", "VD")
                            p.op("se", "copy", zsc[:, c2, :], bk, inc="EX")
                            p.wait("ve", "EX")
                            if e == 0:
                                p.op("ve", "tensor_add", SCR[:, dt_, csl], zsc[:, c2, :],
                                     B1[:, dt_, csl], inc="VD")
                            else:
                                p.op("ve", "tensor_tensor", SCR[:, dt_, csl],
                                     SCR[:, dt_, csl], zsc[:, c2, :], op=ALU.add, inc="VD")
                            banktok[(dt_ % 2, c2)] = p.n["VD"]
                        dttok.append(p.n["MM"])
                    last_w2 = p.n["MM"]
            return joff

        def l1ffn_phase():
            off0 = 0
            wstg, off0 = carve(off0, (128, 2, 16, 128))
            xa_sb, off0 = carve(off0, (128, 16, 128))
            lb_sb, off0 = carve(off0, (128, S))
            xaw, off0 = carve(off0, (128, S))
            wbc, off0 = carve(off0, (128, CH))
            joff = off0
            go = 0
            glr, go = carve(go, (128, 2048), F32)
            gtok, go = carve(go, (128, 128), F32)
            gt_e1, go = carve(go, (128, 128), F32)
            gt_e2, go = carve(go, (128, 128), F32)
            gt_pw, go = carve(go, (128, 128), F32)
            w7sb_, go = carve(go, (128, S))
            w7sb = w7sb_[0:8, :]
            assert go <= off0 - 512, (go, off0)
            for ch in range(NCH):
                for dt_ in range(16):
                    p.op("te", "matmul", pA[0:8, ch, :], gw1[:, dt_, :],
                         B1[:, dt_, ch * CH:(ch + 1) * CH],
                         start=(dt_ == 0), stop=(dt_ == 15),
                         inc=("MM" if (ch == 3 and dt_ == 15) else None))
            p.wait("ve", "MM")
            glv = glr[0:8, :].rearrange("p (a n) -> p a n", a=4)
            for ch in range(NCH):
                p.op("ve", "tensor_single_scalar", glv[:, ch, :], pA[0:8, ch, :],
                     gb1[:, 0:1], ALU.add, inc=("VD" if ch == 3 else None))
            p.wait("te", "VD")
            for m in range(16):
                p.op("te", "transpose", pB[:, 2, m * 8:(m + 1) * 8],
                     glv[:, m // 4, (m % 4) * 128:(m % 4 + 1) * 128], id_f[0:8, 0:8],
                     inc=("MM" if m == 15 else None))
            p.wait("ve", "MM")
            p.vop("tensor_copy", gtok, pB[:, 2, 0:128])
            gv = lambda t, m: t[:, m * 8:(m + 1) * 8]
            for m in range(16):
                p.vop("reduce_max", gm[:, m:m + 1], gv(gtok, m), axis=AX.X)
                p.vop("tensor_single_scalar", gv(gt_e1, m), gv(gtok, m),
                      gm[:, m:m + 1], ALU.is_equal)
                p.vop("scalar_tensor_tensor", gv(gt_pw, m), gv(gt_e1, m), NEG,
                      gv(gtok, m), op0=ALU.mult, op1=ALU.add)
                p.vop("reduce_max", gm2[:, m:m + 1], gv(gt_pw, m), axis=AX.X)
                p.vop("tensor_single_scalar", gv(gt_e2, m), gv(gt_pw, m),
                      gm2[:, m:m + 1], ALU.is_equal)
            p.vop("tensor_sub", gm2, gm2, gm)
            p.wait("se", "VD")
            p.op("se", "activation", gm2, gm2, AF.Exp, bias=0.0, scale=1.0, inc="EX")
            p.wait("ve", "EX")
            p.vop("tensor_single_scalar", gm, gm2, 1.0, ALU.add)
            p.vop("reciprocal", gm, gm)
            p.vop("tensor_mul", gm2, gm2, gm)
            for m in range(16):
                p.vop("tensor_single_scalar", gv(gt_e1, m), gv(gt_e1, m),
                      gm[:, m:m + 1], ALU.mult)
                p.vop("scalar_tensor_tensor", gv(gtok, m), gv(gt_e2, m),
                      gm2[:, m:m + 1], gv(gt_e1, m), op0=ALU.mult, op1=ALU.add)
            p.wait("te", "VD")
            for m in range(16):
                p.op("te", "transpose", pA[0:8, m // 4, (m % 4) * 128:(m % 4 + 1) * 128],
                     gv(gtok, m), id_f, inc=("MM" if m == 15 else None))
            p.wait("ve", "MM")
            p.op("ve", "tensor_copy", w7sb,
                 pA[0:8, :, :].rearrange("p a n -> p (a n)"), inc="VD")
            p.wait("sy", "VD")
            p.dma("sy", wrow_scr, w7sb, "ST")
            p.wait("sy", "ST")
            p.wait("ge", "ST")
            lat = p.dma("ge", xa_sb, dram["lA"], "WKV")
            lbt = p.dma("ge", lb_sb, dram["lB"], "WKV")
            p.wait("te", "WKV", lat)
            reptok = None
            for ch in range(NCH):
                csl = slice(ch * CH, (ch + 1) * CH)
                if reptok is not None:
                    p.wait("sy", "MM", reptok)
                w8t = p.dma("sy", w8_c, wrow_scr[:, csl], "WR")
                bk = pA[:, ch % 2, :]
                p.wait("te", "VD")
                for dt_ in range(16):
                    p.op("te", "matmul", bk, xa_sb[:, dt_, :], B1[:, dt_, csl],
                         start=(dt_ == 0), stop=(dt_ == 15))
                p.wait("te", "WR", w8t)
                reptok = p.op("te", "matmul", pB[:, 2 + ch % 2, :], repm, w8_c,
                              start=True, stop=True, inc="MM")
                p.wait("se", "MM")
                p.wait("se", "VD")
                p.op("se", "copy", zsc[:, ch % 2, :], bk, inc="EX")
                p.wait("ve", "EX")
                p.vop("tensor_copy", wbc, pB[:, 2 + ch % 2, :])
                p.vop("tensor_tensor", xaw[:, csl], zsc[:, ch % 2, :], wbc, op=ALU.mult)
            p.wait("ge", "ST")
            l1tok = {}
            l1_vd0 = p.n["VD"]
            wft = [p.dma("ge", wstg[:, 0], dram["wf1"][0], "MA")]
            p.wait("te", "WKV", lbt)
            ottok = []
            for ot in range(16):
                slot = ot % 2
                if ot + 1 < 16:
                    if ot >= 1:
                        p.wait("ge", "MM", ottok[ot - 1])
                    wft.append(p.dma("ge", wstg[:, (ot + 1) % 2], dram["wf1"][ot + 1],
                                     "MB" if (ot + 1) % 2 else "MA"))
                p.wait("te", "MB" if ot % 2 else "MA", wft[ot])
                for ch in range(NCH):
                    csl = slice(ch * CH, (ch + 1) * CH)
                    bk = (pA if ot % 2 == 0 else pB)[:, ch % 2, :]
                    p.wait("te", "VD", l1tok.get((ot % 2, ch % 2), l1_vd0))
                    for dt_ in range(16):
                        p.op("te", "matmul", bk, wstg[:, slot, dt_, :], B1[:, dt_, csl],
                             start=(dt_ == 0), stop=False)
                    p.op("te", "matmul", bk, lb_sb[:, ot * 128:(ot + 1) * 128],
                         xaw[:, csl], start=False, stop=True, inc="MM")
                    p.wait("se", "MM")
                    p.wait("se", "VD")
                    p.op("se", "copy", zsc[:, ch % 2, :], bk, inc="EX")
                    p.wait("ve", "EX")
                    p.op("ve", "scalar_tensor_tensor", SCR[:, ot, csl], zsc[:, ch % 2, :],
                         bf1[:, ot:ot + 1], B1[:, ot, csl],
                         op0=ALU.add, op1=ALU.add, inc="VD")
                    l1tok[(ot % 2, ch % 2)] = p.n["VD"]
                ottok.append(p.n["MM"])
            return joff

        dbgz = nc.dram_tensor("dbg_z", [128, 16, S], F16, kind="ExternalOutput").ap() if dbg else None
        dbg_sq = nc.dram_tensor("dbg_sq", [128, 16], F32, kind="ExternalOutput").ap() if dbg else None
        dbg_var = nc.dram_tensor("dbg_var", [128, 16], F32, kind="ExternalOutput").ap() if dbg else None
        dbg_sq_sb = sbA("dbg_sq_sb", (128, 16), F32) if dbg else None
        dbgb1 = nc.dram_tensor("dbg_b1", [128, 16, S], F16, kind="ExternalOutput").ap() if dbg else None
        dbg_ao = nc.dram_tensor("dbg_ao", [128, 16, CH], BF16, kind="ExternalOutput").ap() if dbg else None
        last = phases[-1]
        for ph in phases:
            fin = (ph == last and xout == "out")
            def dbg_dump():
                if not dbg:
                    return
                for t in range(4):
                    p.dma("sy", dbgz[:, 4*t:4*(t+1), :], SCR[:, 4*t:4*(t+1), :], "ST")
                for t in range(4):
                    p.dma("sy", dbgb1[:, 4*t:4*(t+1), :], B1[:, 4*t:4*(t+1), :], "ST")
                p.wait("sy", "ST")
                p.barrier()
            if ph == "l0a":
                attn_phase(dram["wq0"], dram["wk0"], dram["wv0"], 16, 16, 0, dram["wo0"])
                p.barrier()
                dbg_dump()
                join_ln(0, final=fin)
            elif ph == "l0m":
                jo = moe_phase()
                p.barrier()
                dbg_dump()
                join_ln(jo, final=fin)
            elif ph == "l1a":
                attn_phase(dram["wq1"], dram["wk1"], dram["wv1"], 16, 4, 1, dram["wo1"])
                p.barrier()
                dbg_dump()
                join_ln(0, final=fin)
            elif ph == "l1f":
                jo = l1ffn_phase()
                p.barrier()
                dbg_dump()
                join_ln(jo, final=fin)
            p.barrier()
        if xout == "mst":
            p.wait("sy", "VD")
            for t in range(4):
                p.dma("sy", mstout[:, 4 * t:4 * (t + 1), :], B1[:, 4 * t:4 * (t + 1), :], "ST")
            p.wait("sy", "ST")
        p.barrier()
        for e in ENGS:
            p.wait(e, "ST", p.n["ST"])

        lp = nc.allow_low_precision(reason="fp16 residual stream / bf16 operands by design; PSUM accumulation is fp32")
        lp.__enter__()
        with nc.Block() as block:
            @block.tensor
            def _(e): p.replay(e, "te")
            @block.vector
            def _(e): p.replay(e, "ve")
            @block.scalar
            def _(e): p.replay(e, "se")
            @block.gpsimd
            def _(e): p.replay(e, "ge")
            @block.sync
            def _(e): p.replay(e, "sy")
        lp.__exit__(None, None, None)
    return nc


def kernel(**inputs):
    g = host_prep(inputs)
    x = np.asarray(inputs["L0_attn_x"], np.float32)
    nc = build_program()
    in_maps = []
    for b in range(8):
        m = {"d_" + k: v for k, v in g.items()}
        m["d_x"] = prep_x(x[b])
        in_maps.append(m)
    res = run_bass_kernel_spmd(nc, in_maps, core_ids=list(range(8)))
    out = np.stack([unprep_out(r["out"]) for r in res.results])
    return out.astype(np.float32)


# version 05
# speedup vs baseline: 1.1228x; 1.1228x over previous
import os
import numpy as np
import ml_dtypes
from contextlib import ExitStack

import concourse.bass as bass
import concourse.mybir as mybir
from concourse.bass_utils import run_bass_kernel_spmd

F32 = mybir.dt.float32
F16 = mybir.dt.float16
BF16 = mybir.dt.bfloat16
AF = mybir.ActivationFunctionType
ALU = mybir.AluOpType
AX = mybir.AxisListType

S = 2048
NT = 16
NCH = 4
CH = 512
SCALE = float(1.0 / np.sqrt(128.0))
EPS = 1e-5
NEG = -1e30

ENGS = ("te", "ve", "se", "ge", "sy")


class Prog:
    def __init__(self, nc, sems):
        self.nc = nc
        self.ops = {e: [] for e in ENGS}
        self.sems = sems
        self.n = {k: 0 for k in sems}

    def op(self, eng, meth, *args, inc=None, **kw):
        pairs = ()
        r = None
        if inc is not None:
            self.n[inc] += 1
            r = self.n[inc]
            pairs = ((self.sems[inc], 1),)
        self.ops[eng].append((meth, args, kw, pairs))
        return r

    def dma(self, eng, out, in_, inc="WL", **kw):
        self.n[inc] += 16
        self.ops[eng].append(("dma_start", (), dict(out=out, in_=in_, **kw),
                              ((self.sems[inc], 16),)))
        return self.n[inc]

    def wait(self, eng, sem, val=None):
        if val is None:
            val = self.n[sem]
        self.ops[eng].append(("wait", (self.sems[sem], val), {}, ()))

    def vop(self, meth, *args, **kw):
        self.wait("ve", "VD")
        return self.op("ve", meth, *args, inc="VD", **kw)

    def sop(self, meth, *args, **kw):
        self.wait("se", "EX")
        return self.op("se", meth, *args, inc="EX", **kw)

    def barrier(self):
        own = {"te": "MM", "ve": "VD", "se": "EX", "ge": "GP"}
        for e in ENGS:
            if e in own:
                self.wait(e, own[e], self.n[own[e]])
        for e in ENGS:
            self.n["BAR"] += 1
            self.ops[e].append(("inc", (self.sems["BAR"], 1), {}, ()))
        tgt = self.n["BAR"]
        for e in ENGS:
            self.wait(e, "BAR", tgt)

    def replay(self, e, eng):
        for i, (meth, args, kw, inc) in enumerate(self.ops[eng]):
            try:
                if meth == "wait":
                    e.wait_ge(*args)
                elif meth == "inc":
                    e.sem_inc(*args)
                else:
                    inst = getattr(e, meth)(*args, **kw)
                    for s, k in inc:
                        inst.then_inc(s, k)
            except Exception:
                print(f"REPLAY FAIL {eng}[{i}] {meth} args={args} kw={kw}")
                raise



def _wT_heads(W, H):
    e, d = W.shape
    Wr = W.reshape(H, 128, d // 128, 128)
    return np.ascontiguousarray(Wr.transpose(0, 3, 2, 1)).reshape(H, 128, -1)

def _w_in_out(W):
    d, o = W.shape
    Wr = W.reshape(d // 128, 128, o // 128, 128)
    return np.ascontiguousarray(Wr.transpose(2, 1, 0, 3)).reshape(o // 128, 128, -1)

def _wo_dev(W):
    Wr = W.reshape(16, 128, 16, 128)
    return np.ascontiguousarray(Wr.transpose(0, 3, 2, 1)).reshape(16, 128, -1)

def _w2_dev(W):
    Wr = W.reshape(16, 128, 16, 128)
    return np.ascontiguousarray(Wr.transpose(2, 1, 0, 3)).reshape(16, 128, -1)

def _colmat(v):
    d, k = v.shape
    return np.ascontiguousarray(v.reshape(16, 128, k).transpose(1, 0, 2))


def host_prep(inp):
    g = {}
    f32 = np.float32
    A = lambda k: np.asarray(inp[k], f32)
    g["wq0"] = _wT_heads(A("L0_attn_Wq"), 16)
    g["wk0"] = _wT_heads(A("L0_attn_Wk"), 16)
    g["wv0"] = _wT_heads(A("L0_attn_Wv"), 16)
    g["wo0"] = _wo_dev(A("L0_attn_Wo"))
    W1, Wg, W2 = A("L0_ffn_W1"), A("L0_ffn_Wg"), A("L0_ffn_W2")
    g["w1s"] = np.ascontiguousarray(np.stack(
        [_w_in_out(A("L0_ffn_Ws1"))] + [_w_in_out(W1[e]) for e in range(7)]))
    g["wgs"] = np.ascontiguousarray(np.stack(
        [_w_in_out(A("L0_ffn_Wsg"))] + [_w_in_out(Wg[e]) for e in range(7)]))
    g["w2s"] = np.ascontiguousarray(np.stack(
        [_w2_dev(A("L0_ffn_Ws2"))] + [_w2_dev(W2[e]) for e in range(7)]))
    g["gw0"] = _colmat(A("L0_ffn_gate_w"))
    g["gb0"] = A("L0_ffn_gate_b").reshape(7, 1)
    g["bias0"] = A("L0_ffn_biases").reshape(1, 7)
    g["wq1"] = _wT_heads(A("L1_attn_wq"), 16)
    g["wk1"] = _wT_heads(A("L1_attn_wk"), 4)
    g["wv1"] = _wT_heads(A("L1_attn_wv"), 4)
    g["wo1"] = _wo_dev(A("L1_attn_wo"))
    g["wf1"] = _wT_heads(A("L1_ffn_W"), 16)
    g["bf1"] = np.ascontiguousarray(A("L1_ffn_b").reshape(16, 128).T)
    LA, LB = A("L1_ffn_lora_A"), A("L1_ffn_lora_B")
    g["lA"] = _colmat(np.ascontiguousarray(LA.transpose(2, 0, 1)).reshape(2048, 128))
    g["lB"] = np.ascontiguousarray(LB.transpose(0, 2, 1)).reshape(128, 2048)
    g["gw1"] = _colmat(np.ascontiguousarray(A("L1_ffn_gate_w").T))
    g["gb1"] = A("L1_ffn_gate_b").reshape(8, 1)
    inv = 10000.0 ** (-np.arange(64, dtype=np.float64) / 64.0)
    fr = np.outer(np.arange(S, dtype=np.float64), inv)
    c = np.cos(fr).T.astype(f32)
    sn = np.sin(fr).T.astype(f32)
    g["ctab"] = np.concatenate([c, c], 0).astype(np.float16)
    g["stab"] = np.concatenate([sn, sn], 0).astype(np.float16)
    R = np.zeros((128, 128), f32)
    for i in range(64):
        R[i, i + 64] = -1.0
        R[i + 64, i] = 1.0
    g["rmat"] = np.ascontiguousarray(R.T).astype(np.float16)
    g["ident_h"] = np.eye(128, dtype=np.float16)
    g["ident_f"] = np.eye(128, dtype=f32)
    ku = np.arange(128).reshape(-1, 1)
    u0 = np.arange(896).reshape(1, -1)
    g["mask0"] = (u0 >= ku + 384).astype(ml_dtypes.bfloat16)
    v1 = np.arange(1536).reshape(1, -1)
    g["mask1"] = ((v1 >= 512 + ku) & (v1 <= 1024 + ku)).astype(ml_dtypes.bfloat16)
    g["ones_h"] = np.ones((128, 128), np.float16)
    g["ones_b"] = np.ones((128, 128), ml_dtypes.bfloat16)
    g["ones_f"] = np.ones((128, 128), f32)
    rep = np.zeros((8, 128), f32)
    for e in range(8):
        rep[e, e * 16:(e + 1) * 16] = 1.0
    g["rep"] = rep.astype(np.float16)
    return g


def prep_x(x_b):
    return np.ascontiguousarray(np.asarray(x_b, np.float32).T.reshape(16, 128, S).transpose(1, 0, 2))


def unprep_out(dev):
    return np.ascontiguousarray(np.asarray(dev).transpose(1, 0, 2).reshape(2048, S).T)



def build_program(phases=("l0a", "l0m", "l1a", "l1f"), xin="x", xout="out", dbg=False):
    nc = bass.Bass("TRN2", target_bir_lowering=False)

    dram = {}
    def din(name, shape, dt):
        dram[name] = nc.dram_tensor("d_" + name, list(shape), dt, kind="ExternalInput").ap()
        return dram[name]

    dx = din("x", (128, 16, S), F32) if xin == "x" else None
    mstin = din("mstin", (128, 16, S), F16) if xin == "mst" else None
    dout = nc.dram_tensor("out", [128, 16, S], F32, kind="ExternalOutput").ap() \
        if xout == "out" else None
    mstout = nc.dram_tensor("mstout", [128, 16, S], F16, kind="ExternalOutput").ap() \
        if xout == "mst" else None

    din("wq0", (16, 128, 2048), F32); din("wk0", (16, 128, 2048), F32)
    din("wv0", (16, 128, 2048), F32); din("wo0", (16, 128, 2048), F32)
    din("w1s", (8, 16, 128, 2048), F32); din("wgs", (8, 16, 128, 2048), F32)
    din("w2s", (8, 16, 128, 2048), F32)
    din("gw0", (128, 16, 7), F32); din("gb0", (7, 1), F32); din("bias0", (1, 7), F32)
    din("wq1", (16, 128, 2048), F32); din("wk1", (4, 128, 2048), F32)
    din("wv1", (4, 128, 2048), F32); din("wo1", (16, 128, 2048), F32)
    din("wf1", (16, 128, 2048), F32); din("bf1", (128, 16), F32)
    din("lA", (128, 16, 128), F32); din("lB", (128, 2048), F32)
    din("gw1", (128, 16, 8), F32); din("gb1", (8, 1), F32)
    din("ctab", (128, S), F16); din("stab", (128, S), F16)
    din("rmat", (128, 128), F16); din("ident_h", (128, 128), F16)
    din("ident_f", (128, 128), F32)
    din("mask0", (128, 896), BF16); din("mask1", (128, 1536), BF16)
    din("ones_h", (128, 128), F16); din("ones_b", (128, 128), BF16)
    din("ones_f", (128, 128), F32); din("rep", (8, 128), F16)

    attn_scr = nc.dram_tensor("attn_scr", [16, 128, S], BF16).ap()
    bridge_scr = nc.dram_tensor("bridge_scr", [32, 128], F16).ap()
    wrow_scr = nc.dram_tensor("wrow_scr", [8, S], F16).ap()

    st = ExitStack()
    with st:
        sbA = lambda name, shape, dt: st.enter_context(nc.sbuf_tensor(name, list(shape), dt))[:]
        psA = lambda name, shape, dt: st.enter_context(nc.psum_tensor(name, list(shape), dt))[:]

        B1 = sbA("B1", (128, 16, S), F16)
        B2r = sbA("B2", (128, 30 * 1024), F16)
        SCR = sbA("SCR", (128, 16, S), F16)
        ctab = sbA("ctab", (128, S), F16); stab = sbA("stab", (128, S), F16)
        rmat = sbA("rmat", (128, 128), F16)
        id_h = sbA("id_h", (128, 128), F16); id_f = sbA("id_f", (128, 128), F32)
        ones_h = sbA("ones_h", (128, 128), F16); ones_b = sbA("ones_b", (128, 128), BF16)
        ones_f = sbA("ones_f", (128, 128), F32); repm = sbA("rep", (8, 128), F16)
        gw0 = sbA("gw0", (128, 16, 7), F16); gb0 = sbA("gb0", (7, 1), F32)
        bias0r = sbA("bias0r", (1, 7), F32); bias0b = sbA("bias0b", (128, 7), F32)
        gw1 = sbA("gw1", (128, 16, 8), F16); gb1 = sbA("gb1", (8, 1), F32)
        bf1 = sbA("bf1", (128, 16), F32)
        brcol = sbA("brcol", (128, 32), F32)
        br2 = sbA("br2", (32, 128), F16)
        mr_c = sbA("mr_c", (1, CH), F16)
        rr_c = sbA("rr_c", (1, CH), F16)
        wrow_c = sbA("wrow_c", (1, 1024), F16)
        w8_c = sbA("w8_c", (8, CH), F16)
        sc_a = sbA("sc_a", (1, 16), F32)
        biash = sbA("biash", (128, 1), F32)
        lr_sb = sbA("lr_sb", (1, CH), BF16)
        gm = sbA("gm", (128, 16), F32); gm2 = sbA("gm2", (128, 16), F32)
        stat_sb = sbA("stat_sb", (128, 32), F32)
        zsc = sbA("zsc", (128, 2, CH), F16)
        epsb = sbA("epsb", (128, 1), F32)

        pA = psA("pA", (128, 4, CH), F32)
        pB = psA("pB", (128, 4, CH), F32)
        pAh = pA.bitcast(F16)
        pBh = pB.bitcast(F16)

        semnames = ["WL", "XL", "ST", "MM", "EX", "VD", "GP", "BAR",
                    "WQ", "WKV", "AO0", "AO1", "WO0", "WO1", "ROW", "WR",
                    "MA", "MB", "SP", "OT0", "OT1", "OT2", "OT3"]
        sems = {k: st.enter_context(nc.semaphore(k)) for k in semnames}
        p = Prog(nc, sems)

        def carve(off, shape, dt=F16):
            n = int(np.prod(shape[1:]))
            scale = {F16: 1, BF16: 1}.get(dt, 2)
            v = B2r[:, off:off + n * scale]
            if dt != F16:
                v = v.bitcast(dt)
            if len(shape) > 2:
                mid = shape[1:-1]
                dims = " ".join(f"d{i}" for i in range(len(mid)))
                v = v.rearrange(f"p ({dims} n) -> p {dims} n",
                                **{f"d{i}": mid[i] for i in range(len(mid))})
            return v, off + n * scale

        for nm, dst, eng in [("ctab", ctab, "sy"), ("stab", stab, "sy"),
                             ("rmat", rmat, "sy"), ("ident_h", id_h, "sy"),
                             ("ident_f", id_f, "sy"), ("ones_h", ones_h, "sy"),
                             ("ones_b", ones_b, "sy"), ("ones_f", ones_f, "sy"),
                             ("rep", repm, "sy"), ("gb0", gb0, "sy"),
                             ("bias0", bias0r, "sy"), ("gb1", gb1, "sy"),
                             ("bf1", bf1, "sy")]:
            p.dma(eng, dst, dram[nm], "XL")
        p.dma("ge", gw0, dram["gw0"], "WL")
        p.dma("ge", gw1, dram["gw1"], "WL")
        if xin == "x":
            for t in range(4):
                p.dma("ge", B1[:, 4 * t:4 * (t + 1), :], dx[:, 4 * t:4 * (t + 1), :], "WL")
        else:
            for t in range(4):
                p.dma("sy", B1[:, 4 * t:4 * (t + 1), :], mstin[:, 4 * t:4 * (t + 1), :], "XL")
        for e in ENGS:
            p.wait(e, "XL")
            p.wait(e, "WL")
        p.op("ve", "memset", epsb, EPS)
        p.op("te", "matmul", pA[:, 0, 0:7], ones_f[0:1, :], bias0r, start=True, stop=True, inc="MM")
        p.wait("ve", "MM")
        p.op("ve", "tensor_copy", bias0b, pA[:, 0, 0:7], inc="VD")
        p.wait("te", "VD")

        def bridge(cols_ap, nelem, outs):
            k = cols_ap.shape[-1]
            p.wait("te", "VD")
            p.wait("te", "EX")
            p.op("te", "transpose", pB[0:k, 2, 0:128], cols_ap, id_f, inc="MM")
            p.wait("ve", "MM")
            p.op("ve", "tensor_copy", br2[0:k, :], pB[0:k, 2, 0:128], inc="VD")
            p.wait("sy", "VD")
            p.dma("sy", bridge_scr[0:k, :], br2[0:k, :], "ST")
            p.wait("sy", "ST")
            flat = bridge_scr.rearrange("a b -> (a b)").unsqueeze(0)
            tok = None
            for dst, offe in outs:
                n = dst.shape[-1]
                tok = p.dma("sy", dst, flat[:, offe:offe + n], "XL")
            return tok

        def join_ln(off0, final=False):
            if os.environ.get("JOIN_SKIP"):
                return
            z2d, off = carve(0, (128, 2, S))
            meanb, off = carve(off, (128, CH))
            rstdb, off = carve(off, (128, CH))
            tnorm, off = carve(off, (128, CH))
            mrow_v, off = carve(off, (128, S))
            rrow_v, off = carve(off, (128, S))
            mrow = mrow_v[0:1, :]
            rrow = rrow_v[0:1, :]
            if final:
                xst, off = carve(off, (128, 4, CH), F32)
            assert off <= 30 * 1024, off
            zt = lambda m: SCR[:, :, m * 128:(m + 1) * 128]
            sqtok = []
            z2tok = []
            p.wait("te", "VD")
            p.wait("te", "EX")
            for m in range(16):
                if m >= 2:
                    p.wait("ve", "MM", sqtok[m - 2])
                p.op("ve", "tensor_tensor",
                     z2d[:, m % 2].rearrange("p (t n) -> p t n", t=16), zt(m), zt(m),
                     op=ALU.mult, inc="VD")
                z2tok.append(p.n["VD"])
                for dt_ in range(16):
                    p.op("te", "matmul", pB[:, 0, m:m + 1], SCR[:, dt_, m * 128:(m + 1) * 128],
                         ones_h[:, 0:1], start=(dt_ == 0), stop=(dt_ == 15))
                p.wait("te", "VD", z2tok[m])
                for dt_ in range(16):
                    t = p.op("te", "matmul", pB[:, 1, m:m + 1],
                             z2d[:, m % 2, dt_ * 128:(dt_ + 1) * 128], ones_h[:, 0:1],
                             start=(dt_ == 0), stop=(dt_ == 15),
                             inc=("MM" if dt_ == 15 else None))
                sqtok.append(t)
            p.wait("ve", "MM")
            p.vop("tensor_copy", stat_sb[:, 0:16], pB[:, 0, 0:16])
            p.vop("tensor_copy", stat_sb[:, 16:32], pB[:, 1, 0:16])
            p.vop("tensor_single_scalar", brcol[:, 0:16], stat_sb[:, 0:16], 1.0 / 2048, ALU.mult)
            p.vop("tensor_mul", gm, brcol[:, 0:16], brcol[:, 0:16])
            p.vop("scalar_tensor_tensor", gm2, stat_sb[:, 16:32], 1.0 / 2048, gm,
                 op0=ALU.mult, op1=ALU.subtract)
            if dbg:
                p.wait("sy", "VD")
                p.dma("sy", dbg_var, gm2, "ST")
            p.wait("se", "VD")
            p.sop("activation", gm, gm2, AF.Ln, bias=epsb[:, 0:1], scale=1.0)
            p.sop("activation", brcol[:, 16:32], gm, AF.Exp, bias=0.0, scale=-0.5)
            p.wait("ve", "EX")
            bridge(brcol[:, 0:32], 32, [])
            sttok = []
            flat = bridge_scr.rearrange("a b -> (a b)").unsqueeze(0)
            p.dma("sy", mrow, flat[:, 0:2048], "ROW")
            rowt = p.dma("sy", rrow, flat[:, 2048:4096], "ROW")
            p.wait("te", "ROW", rowt)
            for ch in range(NCH):
                csl = slice(ch * CH, (ch + 1) * CH)
                p.wait("te", "VD")
                p.op("te", "matmul", pA[:, 0, :], ones_h[0:1, :], mrow[:, csl], start=True, stop=True)
                p.op("te", "matmul", pA[:, 1, :], ones_h[0:1, :], rrow[:, csl], start=True, stop=True, inc="MM")
                p.wait("ve", "MM")
                p.op("ve", "tensor_copy", meanb, pA[:, 0, :])
                p.op("ve", "tensor_copy", rstdb, pA[:, 1, :], inc="VD")
                for dt_ in range(16):
                    p.op("ve", "tensor_sub", tnorm, SCR[:, dt_, csl], meanb)
                    if final:
                        sl = dt_ % 4
                        osem = f"OT{sl}"
                        if ch * 16 + dt_ >= 4:
                            p.wait("ve", osem, sttok[ch * 16 + dt_ - 4])
                        p.op("ve", "tensor_mul", xst[:, sl, :], tnorm, rstdb, inc="VD")
                        p.wait("sy", "VD")
                        sttok.append(p.dma("sy", dout[:, dt_, csl], xst[:, sl, :], osem))
                    else:
                        p.op("ve", "tensor_mul", B1[:, dt_, csl], tnorm, rstdb,
                             inc=("VD" if dt_ == 15 else None))
            if final:
                for q_ in range(4):
                    p.wait("sy", f"OT{q_}")

        def attn_phase(wq_d, wk_d, wv_d, nH, nG, msel, wo_d):
            rep_ = nH // nG
            off = 0
            Pbuf, off = carve(off, (128, 16, CH), BF16)
            qt, off = carve(off, (128, S))
            kt, off = carve(off, (128, S))
            qraw, off = carve(off, (128, S))
            rot, off = carve(off, (128, S))
            kraw, off = carve(off, (128, S))
            vn, off = carve(off, (128, 16, 128), BF16)
            aout, off = carve(off, (128, S), BF16)
            wqs, off = carve(off, (128, 16, 128))
            wks, off = carve(off, (128, 16, 128))
            wvs, off = carve(off, (128, 16, 128))
            msk, off = carve(off, (128, 1536), BF16)
            recb, off = carve(off, (128, CH), BF16)
            lrow_s = lr_sb
            assert off <= 30 * 1024, off
            Pbuf2 = SCR.rearrange("p a n -> p (a n)").bitcast(BF16).rearrange("p (t n) -> p t n", t=64)[:, 0:16, :]
            Pring = [Pbuf, Pbuf2]
            mask = msk[:, 0:896] if msel == 0 else msk

            mld = p.dma("sy", msk[:, 0:896] if msel == 0 else msk,
                        dram["mask0"] if msel == 0 else dram["mask1"], "XL")
            p.wait("ge", "XL", mld)

            def allowed(j):
                if msel == 0:
                    return list(range(0, 4 * j + 4)), set(range(4 * j, 4 * j + 4))
                lo = max(0, 4 * j - 4)
                kts = list(range(lo, 4 * j + 4))
                return kts, set(kts)

            def mask_ap(j, kt_):
                dlt = 512 * j - 128 * kt_
                o = dlt + 384 if msel == 0 else dlt + 512
                return mask[:, o:o + 512]

            heads = [(h, h // rep_) for h in range(nH)]
            qtok = [p.dma("ge", wqs, wq_d[0], "WQ")]
            p.dma("ge", wks, wk_d[0], "WKV")
            p.dma("ge", wvs, wv_d[0], "WKV")
            kvtok = [p.n["WKV"]]

            def mm64(bank4, wsb, tok, sem):
                p.wait("te", sem, tok)
                for ch in range(NCH):
                    for dt_ in range(16):
                        p.op("te", "matmul", bank4[:, ch, :], wsb[:, dt_, :],
                             B1[:, dt_, ch * CH:(ch + 1) * CH],
                             start=(dt_ == 0), stop=(dt_ == 15),
                             inc=("MM" if (ch == 3 and dt_ == 15) else None))
                return p.n["MM"]

            def evac(dst, bank4):
                p.wait("se", "MM")
                return p.op("se", "copy", dst, bank4.rearrange("p a n -> p (a n)"), inc="EX")

            for hi, (h, g) in enumerate(heads):
                newg = hi == 0 or g != heads[hi - 1][1]
                p.wait("te", "EX")
                p.wait("te", "VD")
                mm64(pA, wqs, qtok[hi], "WQ")
                evac(qraw, pA)
                if newg:
                    mm64(pB, wks, kvtok[g], "WKV")
                    evac(kraw, pB)
                qraw_tok = p.n["EX"] - (1 if newg else 0)
                p.wait("te", "EX", qraw_tok)
                for ch in range(NCH):
                    p.op("te", "matmul", pA[:, ch, :], rmat, qraw[:, ch * CH:(ch + 1) * CH],
                         start=True, stop=True, inc=("MM" if ch == 3 else None))
                rq_tok = evac(rot, pA)
                if newg:
                    p.wait("te", "EX", rq_tok - 1)
                    for ch in range(NCH):
                        p.op("te", "matmul", pB[:, ch, :], rmat, kraw[:, ch * CH:(ch + 1) * CH],
                             start=True, stop=True, inc=("MM" if ch == 3 else None))
                p.wait("ve", "EX", rq_tok)
                p.op("ve", "tensor_mul", qraw, qraw, ctab)
                p.op("ve", "tensor_mul", rot, rot, stab)
                p.op("ve", "tensor_add", qt, qraw, rot, inc="VD")
                ropeq_tok = p.n["VD"]
                if newg:
                    p.wait("se", "VD", ropeq_tok)
                    rk_tok = evac(rot, pB)
                    p.wait("te", "EX", rq_tok)
                    mm64(pA, wvs, kvtok[g], "WKV")
                    p.wait("se", "VD", ropeq_tok)
                    vt_tok = evac(qraw, pA)
                    p.wait("ve", "EX", rk_tok)
                    p.op("ve", "tensor_mul", kraw, kraw, ctab)
                    p.op("ve", "tensor_mul", rot, rot, stab)
                    p.op("ve", "tensor_add", kt, kraw, rot, inc="VD")
                    p.wait("te", "EX", vt_tok)
                    for t4 in range(4):
                        for i4 in range(4):
                            kk = t4 * 4 + i4
                            p.op("te", "transpose", pBh[:, t4, i4 * 128:(i4 + 1) * 128],
                                 qraw[:, kk * 128:(kk + 1) * 128], id_h,
                                 inc=("MM" if i4 == 3 else None))
                        p.wait("ve", "MM")
                        p.op("ve", "tensor_copy",
                             vn[:, t4 * 4:(t4 + 1) * 4, :].rearrange("p a n -> p (a n)"),
                             pBh[:, t4, 0:512], inc="VD")
                mm_proj = p.n["MM"]
                if hi + 1 < nH:
                    p.wait("ge", "MM", mm_proj)
                    qtok.append(p.dma("ge", wqs, wq_d[hi + 1], "WQ"))
                    if heads[hi + 1][1] != g:
                        p.dma("ge", wks, wk_d[g + 1], "WKV")
                        p.dma("ge", wvs, wv_d[g + 1], "WKV")
                        kvtok.append(p.n["WKV"])
                p.wait("ve", "VD")
                p.wait("ve", "MM")
                p.op("ve", "tensor_mul", qraw, qt, qt)
                p.op("ve", "tensor_mul", rot, kt, kt, inc="VD")
                p.wait("te", "VD")
                for m in range(16):
                    p.op("te", "matmul", pB[:, 0, m:m + 1], qraw[:, m * 128:(m + 1) * 128],
                         ones_h[:, 0:1], start=True, stop=True)
                for m in range(16):
                    p.op("te", "matmul", pB[:, 1, m:m + 1], rot[:, m * 128:(m + 1) * 128],
                         ones_h[:, 0:1], start=True, stop=True,
                         inc=("MM" if m == 15 else None))
                p.wait("ve", "MM")
                p.vop("reduce_max", brcol[:, 0:1], pB[:, 0, 0:16], axis=AX.X)
                p.vop("reduce_max", brcol[:, 1:2], pB[:, 1, 0:16], axis=AX.X)
                p.vop("reduce_max", brcol[:, 2:3], brcol[:, 0:2], axis=AX.X)
                p.wait("te", "VD")
                p.op("te", "transpose", pB[0:1, 2, 0:128], brcol[:, 2:3], id_f, inc="MM")
                p.wait("ve", "MM")
                p.vop("reduce_max", sc_a[:, 0:1], pB[0:1, 2, 0:128], axis=AX.X)
                p.vop("tensor_single_scalar", sc_a[:, 1:2], sc_a[:, 0:1],
                      -1.02 * SCALE, ALU.mult)
                p.wait("te", "VD")
                p.op("te", "matmul", pB[:, 3, 0:1], ones_f[0:1, :], sc_a[:, 1:2],
                     start=True, stop=True, inc="MM")
                p.wait("ve", "MM")
                p.wait("ve", "EX")
                p.op("ve", "tensor_copy", biash, pB[:, 3, 0:1], inc="VD")
                bias_vd = p.n["VD"]
                p.wait("se", "VD")

                flash_ex0 = p.n["EX"]
                extok = {}
                gptok = {}
                sbank_ex = {}
                pb_first = [True]
                amtok = [bias_vd]

                def emit_S(j):
                    kts, partial = allowed(j)
                    qsl = slice(j * CH, (j + 1) * CH)
                    P = Pring[j % 2]
                    nb = (len(kts) + 2) // 3
                    for b in range(nb):
                        bank = pA if b % 2 == 0 else pB
                        sub = kts[b * 3:(b + 1) * 3]
                        p.wait("te", "EX", sbank_ex.get(b % 2, flash_ex0))
                        if b % 2 == 1 and pb_first[0]:
                            p.wait("te", "VD", bias_vd)
                            pb_first[0] = False
                        for i, kt_ in enumerate(sub):
                            p.op("te", "matmul", bank[:, i, :], kt[:, kt_ * 128:(kt_ + 1) * 128],
                                 qt[:, qsl], start=True, stop=True,
                                 inc=("MM" if i == len(sub) - 1 else None))
                        p.wait("se", "MM")
                        for i, kt_ in enumerate(sub):
                            p.op("se", "activation", P[:, kt_ % 16, :], bank[:, i, :],
                                 AF.Exp, bias=biash[:, 0:1], scale=SCALE,
                                 inc=("EX" if i == len(sub) - 1 else None))
                        ex_now = p.n["EX"]
                        sbank_ex[b % 2] = ex_now
                        for i, kt_ in enumerate(sub):
                            if kt_ in partial:
                                p.wait("ve", "EX", ex_now)
                                p.op("ve", "tensor_mul", P[:, kt_ % 16, :],
                                     P[:, kt_ % 16, :], mask_ap(j, kt_), inc="VD")
                    extok[j] = p.n["EX"]
                    gptok[j] = p.n["VD"]

                def emit_tail(j):
                    kts, _ = allowed(j)
                    qsl = slice(j * CH, (j + 1) * CH)
                    P = Pring[j % 2]
                    p.wait("te", "EX", extok[j])
                    p.wait("te", "VD", max(gptok[j], amtok[0]))
                    for i, kt_ in enumerate(kts):
                        p.op("te", "matmul", pB[0:1, 3, :], ones_b[:, 0:1], P[:, kt_ % 16, :],
                             start=(i == 0), stop=(i == len(kts) - 1),
                             inc=("MM" if i == len(kts) - 1 else None))
                    mm_l = p.n["MM"]
                    for i, kt_ in enumerate(kts):
                        p.op("te", "matmul", pA[:, 3, :], vn[:, kt_, :], P[:, kt_ % 16, :],
                             start=(i == 0), stop=(i == len(kts) - 1),
                             inc=("MM" if i == len(kts) - 1 else None))
                    p.wait("ve", "MM", mm_l)
                    p.op("ve", "tensor_copy", lrow_s, pB[0:1, 3, :], inc="VD")
                    p.wait("te", "VD")
                    p.op("te", "matmul", pB[:, 3, :], ones_b[0:1, :], lrow_s,
                         start=True, stop=True, inc="MM")
                    p.wait("se", "MM")
                    p.op("se", "copy", aout[:, qsl], pA[:, 3, :], inc="EX")
                    p.wait("ve", "EX")
                    p.wait("ve", "MM")
                    p.op("ve", "reciprocal", recb, pB[:, 3, :])
                    p.op("ve", "tensor_mul", aout[:, qsl], aout[:, qsl], recb, inc="VD")
                    amtok[0] = p.n["VD"]

                emit_S(0)
                for j in range(1, NCH):
                    emit_S(j)
                    emit_tail(j - 1)
                emit_tail(NCH - 1)
                p.wait("sy", "VD")
                p.dma("sy", attn_scr[h], aout, "SP")

            p.barrier()
            off = 0
            ao, off = carve(off, (128, 2, 16, CH), BF16)
            wos, off = carve(off, (128, 2, 16, 128), BF16)
            assert off <= 30 * 1024
            a_t = attn_scr.rearrange("h p n -> p h n")
            p.wait("sy", "SP")
            p.wait("ge", "SP")
            aotok = [p.dma("sy", ao[:, 0], a_t[:, :, 0:CH], "AO0")]
            wotok = [p.dma("ge", wos[:, 0], wo_d[0], "WO0")]
            mmtok = []
            for ch in range(NCH):
                if ch + 1 < NCH:
                    if ch >= 1:
                        p.wait("sy", "MM", mmtok[ch * 16 - 1])
                    aotok.append(p.dma("sy", ao[:, (ch + 1) % 2],
                                       a_t[:, :, (ch + 1) * CH:(ch + 2) * CH],
                                       "AO1" if (ch + 1) % 2 else "AO0"))
                for dt_ in range(16):
                    widx = ch * 16 + dt_
                    if widx + 1 < 64:
                        if widx >= 1:
                            p.wait("ge", "MM", mmtok[widx - 1])
                        wotok.append(p.dma("ge", wos[:, (widx + 1) % 2],
                                           wo_d[(widx + 1) % 16],
                                           "WO1" if (widx + 1) % 2 else "WO0"))
                    p.wait("te", "AO1" if ch % 2 else "AO0", aotok[ch])
                    p.wait("te", "WO1" if widx % 2 else "WO0", wotok[widx])
                    p.wait("te", "VD")
                    bank = pA[:, widx % 2, :]
                    for hh in range(16):
                        p.op("te", "matmul", bank, wos[:, widx % 2, hh, :],
                             ao[:, ch % 2, hh, :], start=(hh == 0), stop=(hh == 15),
                             inc=("MM" if hh == 15 else None))
                    mmtok.append(p.n["MM"])
                    p.wait("se", "MM")
                    p.wait("se", "VD")
                    p.op("se", "copy", zsc[:, widx % 2, :], bank, inc="EX")
                    p.wait("ve", "EX")
                    p.op("ve", "tensor_add", SCR[:, dt_, ch * CH:(ch + 1) * CH],
                         zsc[:, widx % 2, :], B1[:, dt_, ch * CH:(ch + 1) * CH], inc="VD")
                    if dbg and ch == 0 and dt_ == 15:
                        p.dma("sy", dbg_ao, ao[:, 0], "ST")

        def moe_phase():
            off0 = 0
            actp, off0 = carve(off0, (128, 16, 1024), BF16)
            wstg, off0 = carve(off0, (128, 2, 2, 16, 128))
            w2st = wstg.bitcast(BF16)
            wb, off0 = carve(off0, (128, 2, CH), BF16)
            sil, off0 = carve(off0, (128, CH), BF16)
            sil2, off0 = carve(off0, (128, CH), BF16)
            gtok, off0 = carve(off0, (128, 128), F32)
            w7sb_, off0 = carve(off0, (128, S))
            w7sb = w7sb_[0:8, :]
            joff = off0
            vd_gate_l = [0]
            go = 0
            glr, go = carve(go, (128, 2048), F32)
            gtk2, go = carve(go, (128, 128), F32)
            gt_e1, go = carve(go, (128, 128), F32)
            gt_e2, go = carve(go, (128, 128), F32)
            gt_pw, go = carve(go, (128, 128), F32)
            assert go <= 16 * 1024
            assert off0 <= 30 * 1024, off0

            class _RouteStop(Exception):
                pass
            try:
              if not os.environ.get("MOE_SKIP_ROUTE"):
                for ch in range(NCH):
                    for dt_ in range(16):
                        p.op("te", "matmul", pA[0:7, ch, :], gw0[:, dt_, :],
                             B1[:, dt_, ch * CH:(ch + 1) * CH],
                             start=(dt_ == 0), stop=(dt_ == 15),
                             inc=("MM" if (ch == 3 and dt_ == 15) else None))
                glv = glr[0:7, :].rearrange("p (a n) -> p a n", a=4)
                p.wait("ve", "MM")
                for ch in range(NCH):
                    p.op("ve", "tensor_single_scalar", glv[:, ch, :], pA[0:7, ch, :],
                         gb0[:, 0:1], ALU.add, inc=("VD" if ch == 3 else None))
                p.wait("te", "VD")
                for m in range(16):
                    p.op("te", "transpose", pB[:, 2, m * 8:m * 8 + 7],
                         glv[:, m // 4, (m % 4) * 128:(m % 4 + 1) * 128], id_f[0:7, 0:7],
                         inc=("MM" if m == 15 else None))
                p.wait("ve", "MM")
                p.vop("memset", gtok, NEG)
                p.vop("tensor_copy",
                     gtok.rearrange("p (a n) -> p a n", a=16)[:, :, 0:7],
                     pB[:, 2, 0:128].rearrange("p (a n) -> p a n", a=16)[:, :, 0:7])
                _rp = int(os.environ.get("MOE_ROUTE_PART", "3"))
                if _rp < 2:
                    raise _RouteStop
                p.wait("se", "VD")
                p.op("se", "activation", gtk2, gtok, AF.Exp, bias=0.0, scale=1.0, inc="EX")
                p.wait("ve", "EX")
                gv = lambda t, m: t[:, m * 8:(m + 1) * 8]
                for m in range(16):
                    p.vop("reduce_sum", gm[:, m:m + 1], gv(gtk2, m), axis=AX.X)
                p.vop("reciprocal", gm2, gm)
                for m in range(16):
                    p.vop("tensor_single_scalar", gv(gtk2, m), gv(gtk2, m),
                          gm2[:, m:m + 1], ALU.mult)
                    p.vop("memset", gv(gt_pw, m), NEG)
                    p.vop("tensor_add", gv(gt_pw, m)[:, 0:7], gv(gtk2, m)[:, 0:7], bias0b[:, 0:7])
                    p.vop("reduce_max", gm[:, m:m + 1], gv(gt_pw, m), axis=AX.X)
                    p.vop("tensor_single_scalar", gv(gt_e1, m), gv(gt_pw, m),
                          gm[:, m:m + 1], ALU.is_equal)
                    p.vop("scalar_tensor_tensor", gv(gt_pw, m), gv(gt_e1, m), NEG,
                          gv(gt_pw, m), op0=ALU.mult, op1=ALU.add)
                    p.vop("reduce_max", gm[:, m:m + 1], gv(gt_pw, m), axis=AX.X)
                    p.vop("tensor_single_scalar", gv(gt_e2, m), gv(gt_pw, m),
                          gm[:, m:m + 1], ALU.is_equal)
                    p.vop("scalar_tensor_tensor", gv(gt_pw, m), gv(gt_e2, m), NEG,
                          gv(gt_pw, m), op0=ALU.mult, op1=ALU.add)
                    p.vop("reduce_max", gm[:, m:m + 1], gv(gt_pw, m), axis=AX.X)
                    p.vop("tensor_single_scalar", gv(gt_pw, m), gv(gt_pw, m),
                          gm[:, m:m + 1], ALU.is_equal)
                    p.vop("tensor_add", gv(gt_e1, m), gv(gt_e1, m), gv(gt_e2, m))
                    p.vop("tensor_add", gv(gt_e1, m), gv(gt_e1, m), gv(gt_pw, m))
                    p.vop("tensor_mul", gv(gt_pw, m), gv(gtk2, m), gv(gt_e1, m))
                    p.vop("reduce_sum", gm[:, m:m + 1], gv(gt_pw, m), axis=AX.X)
                    p.vop("reciprocal", gm2[:, m:m + 1], gm[:, m:m + 1])
                    p.vop("tensor_single_scalar", gv(gtok, m), gv(gt_pw, m),
                          gm2[:, m:m + 1], ALU.mult)
                vd_gate_l[0] = p.n["VD"]
            except _RouteStop:
                pass

            if os.environ.get("MOE_STOP") == "route":
                return joff
            _ne = int(os.environ.get("MOE_NE", "8"))
            _nh = int(os.environ.get("MOE_NH", "2"))
            wrtok = None
            last_w2 = None
            wr_store = [0]
            vd_route = p.n["VD"]
            banktok = {}
            for half in range(_nh):
                hb = half * 1024
                for e in range(_ne):
                    if e > 0:
                        if wrtok is not None:
                            p.wait("sy", "MM", wrtok)
                        p.wait("sy", "ST", wr_store[0])
                        wt = p.dma("sy", wrow_c, wrow_scr[e - 1:e, hb:hb + 1024], "WR")
                        p.wait("te", "WR", wt)
                        for c2 in range(2):
                            p.wait("te", "VD", banktok.get((1, 1), vd_route))
                            p.op("te", "matmul", pB[:, 3, :], ones_h[0:1, :],
                                 wrow_c[:, c2 * CH:(c2 + 1) * CH], start=True, stop=True,
                                 inc="MM")
                            p.wait("ve", "MM")
                            wrtok = p.n["MM"]
                            p.op("ve", "tensor_copy", wb[:, c2, :], pB[:, 3, :], inc="VD")
                            banktok[(1, 1)] = p.n["VD"]
                    if last_w2 is not None:
                        p.wait("ge", "MM", last_w2)
                    p.dma("ge", wstg[:, 0, 0], dram["w1s"][e, 0], "MA")
                    p.dma("ge", wstg[:, 0, 1], dram["wgs"][e, 0], "MA")
                    wgt = [p.n["MA"]]
                    httok = []
                    _nht = int(os.environ.get("MOE_NHT", "16"))
                    for Ht in range(_nht):
                        slot = Ht % 2
                        ssem = "MB" if (Ht + 1) % 2 else "MA"
                        if Ht + 1 < _nht:
                            if Ht >= 1:
                                p.wait("ge", "MM", httok[Ht - 1])
                            p.dma("ge", wstg[:, (Ht + 1) % 2, 0],
                                  dram["w1s"][e, Ht + 1], ssem)
                            p.dma("ge", wstg[:, (Ht + 1) % 2, 1],
                                  dram["wgs"][e, Ht + 1], ssem)
                            wgt.append(p.n[ssem])
                        p.wait("te", "MB" if Ht % 2 else "MA", wgt[Ht])
                        for c2 in range(2):
                            csl = slice(hb + c2 * CH, hb + (c2 + 1) * CH)
                            bk = pA if Ht % 2 == 0 else pB
                            p.wait("te", "VD", banktok.get((Ht % 2, c2), vd_route))
                            for dt_ in range(16):
                                p.op("te", "matmul", bk[:, 2 + c2, :], wstg[:, slot, 1, dt_, :],
                                     B1[:, dt_, csl], start=(dt_ == 0), stop=(dt_ == 15))
                            for dt_ in range(16):
                                p.op("te", "matmul", bk[:, c2, :], wstg[:, slot, 0, dt_, :],
                                     B1[:, dt_, csl], start=(dt_ == 0), stop=(dt_ == 15),
                                     inc=("MM" if dt_ == 15 else None))
                            p.wait("se", "MM")
                            p.op("se", "activation", sil, bk[:, c2, :], AF.Silu,
                                 bias=0.0, scale=1.0, inc="EX")
                            p.wait("ve", "EX")
                            p.op("ve", "tensor_tensor", sil2, sil, bk[:, 2 + c2, :], op=ALU.mult)
                            if e == 0:
                                p.op("ve", "tensor_copy", actp[:, Ht, c2 * CH:(c2 + 1) * CH],
                                     sil2, inc="VD")
                            else:
                                p.op("ve", "tensor_mul", actp[:, Ht, c2 * CH:(c2 + 1) * CH],
                                     sil2, wb[:, c2, :], inc="VD")
                            banktok[(Ht % 2, c2)] = p.n["VD"]
                        httok.append(p.n["MM"])
                    if half == 0 and e == 0 and not os.environ.get("MOE_SKIP_ROUTE") \
                            and int(os.environ.get("MOE_ROUTE_PART", "3")) >= 3:
                        gvv = lambda t, m: t[:, m * 8:(m + 1) * 8]
                        p.wait("te", "VD", max(vd_gate_l[0],
                                               banktok.get((0, 0), 0),
                                               banktok.get((0, 1), 0)))
                        for m in range(16):
                            p.op("te", "transpose",
                                 pA[0:8, m // 4, (m % 4) * 128:(m % 4 + 1) * 128],
                                 gvv(gtok, m), id_f, inc=("MM" if m == 15 else None))
                        p.wait("ve", "MM")
                        p.op("ve", "tensor_copy", w7sb,
                             pA[0:8, :, :].rearrange("p a n -> p (a n)"), inc="VD")
                        for _c2 in range(2):
                            banktok[(0, _c2)] = p.n["VD"]
                        p.wait("sy", "VD")
                        wr_store[0] = p.dma("sy", wrow_scr, w7sb, "ST")
                    if os.environ.get("MOE_STOP") == "h1":
                        continue
                    p.wait("ge", "MM", httok[14])
                    w2t = [p.dma("ge", w2st[:, 0, 0], dram["w2s"][e, 0], "MA")]
                    dttok = []
                    vd_act = p.n["VD"]
                    for dt_ in range(16):
                        slot = dt_ % 2
                        ssem = "MB" if (dt_ + 1) % 2 else "MA"
                        if dt_ + 1 < 16:
                            if dt_ >= 1:
                                p.wait("ge", "MM", dttok[dt_ - 1])
                            else:
                                p.wait("ge", "MM", httok[15])
                            w2t.append(p.dma("ge", w2st[:, (dt_ + 1) % 2, 0],
                                             dram["w2s"][e, dt_ + 1], ssem))
                        p.wait("te", "MB" if dt_ % 2 else "MA", w2t[dt_])
                        if dt_ == 0:
                            p.wait("te", "VD", vd_act)
                        for c2 in range(2):
                            csl = slice(hb + c2 * CH, hb + (c2 + 1) * CH)
                            bk = (pA if dt_ % 2 == 0 else pB)[:, c2, :]
                            p.wait("te", "VD", banktok.get((dt_ % 2, c2), vd_route))
                            for Ht in range(16):
                                p.op("te", "matmul", bk, w2st[:, slot, 0, Ht, :],
                                     actp[:, Ht, c2 * CH:(c2 + 1) * CH],
                                     start=(Ht == 0), stop=(Ht == 15),
                                     inc=("MM" if Ht == 15 else None))
                            p.wait("se", "MM")
                            p.wait("se", "VD")
                            p.op("se", "copy", zsc[:, c2, :], bk, inc="EX")
                            p.wait("ve", "EX")
                            if e == 0:
                                p.op("ve", "tensor_add", SCR[:, dt_, csl], zsc[:, c2, :],
                                     B1[:, dt_, csl], inc="VD")
                            else:
                                p.op("ve", "tensor_tensor", SCR[:, dt_, csl],
                                     SCR[:, dt_, csl], zsc[:, c2, :], op=ALU.add, inc="VD")
                            banktok[(dt_ % 2, c2)] = p.n["VD"]
                        dttok.append(p.n["MM"])
                    last_w2 = p.n["MM"]
            return joff

        def l1ffn_phase():
            off0 = 0
            wstg, off0 = carve(off0, (128, 2, 16, 128))
            xa_sb, off0 = carve(off0, (128, 16, 128))
            lb_sb, off0 = carve(off0, (128, S))
            xaw, off0 = carve(off0, (128, S))
            wbc, off0 = carve(off0, (128, CH))
            joff = off0
            go = 0
            glr, go = carve(go, (128, 2048), F32)
            gtok, go = carve(go, (128, 128), F32)
            gt_e1, go = carve(go, (128, 128), F32)
            gt_e2, go = carve(go, (128, 128), F32)
            gt_pw, go = carve(go, (128, 128), F32)
            w7sb_, go = carve(go, (128, S))
            w7sb = w7sb_[0:8, :]
            assert go <= off0 - 512, (go, off0)
            for ch in range(NCH):
                for dt_ in range(16):
                    p.op("te", "matmul", pA[0:8, ch, :], gw1[:, dt_, :],
                         B1[:, dt_, ch * CH:(ch + 1) * CH],
                         start=(dt_ == 0), stop=(dt_ == 15),
                         inc=("MM" if (ch == 3 and dt_ == 15) else None))
            p.wait("ve", "MM")
            glv = glr[0:8, :].rearrange("p (a n) -> p a n", a=4)
            for ch in range(NCH):
                p.op("ve", "tensor_single_scalar", glv[:, ch, :], pA[0:8, ch, :],
                     gb1[:, 0:1], ALU.add, inc=("VD" if ch == 3 else None))
            p.wait("te", "VD")
            for m in range(16):
                p.op("te", "transpose", pB[:, 2, m * 8:(m + 1) * 8],
                     glv[:, m // 4, (m % 4) * 128:(m % 4 + 1) * 128], id_f[0:8, 0:8],
                     inc=("MM" if m == 15 else None))
            p.wait("ve", "MM")
            p.vop("tensor_copy", gtok, pB[:, 2, 0:128])
            gv = lambda t, m: t[:, m * 8:(m + 1) * 8]
            for m in range(16):
                p.vop("reduce_max", gm[:, m:m + 1], gv(gtok, m), axis=AX.X)
                p.vop("tensor_single_scalar", gv(gt_e1, m), gv(gtok, m),
                      gm[:, m:m + 1], ALU.is_equal)
                p.vop("scalar_tensor_tensor", gv(gt_pw, m), gv(gt_e1, m), NEG,
                      gv(gtok, m), op0=ALU.mult, op1=ALU.add)
                p.vop("reduce_max", gm2[:, m:m + 1], gv(gt_pw, m), axis=AX.X)
                p.vop("tensor_single_scalar", gv(gt_e2, m), gv(gt_pw, m),
                      gm2[:, m:m + 1], ALU.is_equal)
            p.vop("tensor_sub", gm2, gm2, gm)
            p.wait("se", "VD")
            p.op("se", "activation", gm2, gm2, AF.Exp, bias=0.0, scale=1.0, inc="EX")
            p.wait("ve", "EX")
            p.vop("tensor_single_scalar", gm, gm2, 1.0, ALU.add)
            p.vop("reciprocal", gm, gm)
            p.vop("tensor_mul", gm2, gm2, gm)
            for m in range(16):
                p.vop("tensor_single_scalar", gv(gt_e1, m), gv(gt_e1, m),
                      gm[:, m:m + 1], ALU.mult)
                p.vop("scalar_tensor_tensor", gv(gtok, m), gv(gt_e2, m),
                      gm2[:, m:m + 1], gv(gt_e1, m), op0=ALU.mult, op1=ALU.add)
            p.wait("te", "VD")
            for m in range(16):
                p.op("te", "transpose", pA[0:8, m // 4, (m % 4) * 128:(m % 4 + 1) * 128],
                     gv(gtok, m), id_f, inc=("MM" if m == 15 else None))
            p.wait("ve", "MM")
            p.op("ve", "tensor_copy", w7sb,
                 pA[0:8, :, :].rearrange("p a n -> p (a n)"), inc="VD")
            p.wait("sy", "VD")
            p.dma("sy", wrow_scr, w7sb, "ST")
            p.wait("sy", "ST")
            p.wait("ge", "ST")
            lat = p.dma("ge", xa_sb, dram["lA"], "WKV")
            lbt = p.dma("ge", lb_sb, dram["lB"], "WKV")
            p.wait("te", "WKV", lat)
            reptok = None
            for ch in range(NCH):
                csl = slice(ch * CH, (ch + 1) * CH)
                if reptok is not None:
                    p.wait("sy", "MM", reptok)
                w8t = p.dma("sy", w8_c, wrow_scr[:, csl], "WR")
                bk = pA[:, ch % 2, :]
                p.wait("te", "VD")
                for dt_ in range(16):
                    p.op("te", "matmul", bk, xa_sb[:, dt_, :], B1[:, dt_, csl],
                         start=(dt_ == 0), stop=(dt_ == 15))
                p.wait("te", "WR", w8t)
                reptok = p.op("te", "matmul", pB[:, 2 + ch % 2, :], repm, w8_c,
                              start=True, stop=True, inc="MM")
                p.wait("se", "MM")
                p.wait("se", "VD")
                p.op("se", "copy", zsc[:, ch % 2, :], bk, inc="EX")
                p.wait("ve", "EX")
                p.vop("tensor_copy", wbc, pB[:, 2 + ch % 2, :])
                p.vop("tensor_tensor", xaw[:, csl], zsc[:, ch % 2, :], wbc, op=ALU.mult)
            p.wait("ge", "ST")
            l1tok = {}
            l1_vd0 = p.n["VD"]
            wft = [p.dma("ge", wstg[:, 0], dram["wf1"][0], "MA")]
            p.wait("te", "WKV", lbt)
            ottok = []
            for ot in range(16):
                slot = ot % 2
                if ot + 1 < 16:
                    if ot >= 1:
                        p.wait("ge", "MM", ottok[ot - 1])
                    wft.append(p.dma("ge", wstg[:, (ot + 1) % 2], dram["wf1"][ot + 1],
                                     "MB" if (ot + 1) % 2 else "MA"))
                p.wait("te", "MB" if ot % 2 else "MA", wft[ot])
                for ch in range(NCH):
                    csl = slice(ch * CH, (ch + 1) * CH)
                    bk = (pA if ot % 2 == 0 else pB)[:, ch % 2, :]
                    p.wait("te", "VD", l1tok.get((ot % 2, ch % 2), l1_vd0))
                    for dt_ in range(16):
                        p.op("te", "matmul", bk, wstg[:, slot, dt_, :], B1[:, dt_, csl],
                             start=(dt_ == 0), stop=False)
                    p.op("te", "matmul", bk, lb_sb[:, ot * 128:(ot + 1) * 128],
                         xaw[:, csl], start=False, stop=True, inc="MM")
                    p.wait("se", "MM")
                    p.wait("se", "VD")
                    p.op("se", "copy", zsc[:, ch % 2, :], bk, inc="EX")
                    p.wait("ve", "EX")
                    p.op("ve", "scalar_tensor_tensor", SCR[:, ot, csl], zsc[:, ch % 2, :],
                         bf1[:, ot:ot + 1], B1[:, ot, csl],
                         op0=ALU.add, op1=ALU.add, inc="VD")
                    l1tok[(ot % 2, ch % 2)] = p.n["VD"]
                ottok.append(p.n["MM"])
            return joff

        dbgz = nc.dram_tensor("dbg_z", [128, 16, S], F16, kind="ExternalOutput").ap() if dbg else None
        dbg_sq = nc.dram_tensor("dbg_sq", [128, 16], F32, kind="ExternalOutput").ap() if dbg else None
        dbg_var = nc.dram_tensor("dbg_var", [128, 16], F32, kind="ExternalOutput").ap() if dbg else None
        dbg_sq_sb = sbA("dbg_sq_sb", (128, 16), F32) if dbg else None
        dbgb1 = nc.dram_tensor("dbg_b1", [128, 16, S], F16, kind="ExternalOutput").ap() if dbg else None
        dbg_ao = nc.dram_tensor("dbg_ao", [128, 16, CH], BF16, kind="ExternalOutput").ap() if dbg else None
        last = phases[-1]
        for ph in phases:
            fin = (ph == last and xout == "out")
            def dbg_dump():
                if not dbg:
                    return
                for t in range(4):
                    p.dma("sy", dbgz[:, 4*t:4*(t+1), :], SCR[:, 4*t:4*(t+1), :], "ST")
                for t in range(4):
                    p.dma("sy", dbgb1[:, 4*t:4*(t+1), :], B1[:, 4*t:4*(t+1), :], "ST")
                p.wait("sy", "ST")
                p.barrier()
            if ph == "l0a":
                attn_phase(dram["wq0"], dram["wk0"], dram["wv0"], 16, 16, 0, dram["wo0"])
                p.barrier()
                dbg_dump()
                join_ln(0, final=fin)
            elif ph == "l0m":
                jo = moe_phase()
                p.barrier()
                dbg_dump()
                join_ln(jo, final=fin)
            elif ph == "l1a":
                attn_phase(dram["wq1"], dram["wk1"], dram["wv1"], 16, 4, 1, dram["wo1"])
                p.barrier()
                dbg_dump()
                join_ln(0, final=fin)
            elif ph == "l1f":
                jo = l1ffn_phase()
                p.barrier()
                dbg_dump()
                join_ln(jo, final=fin)
            p.barrier()
        if xout == "mst":
            p.wait("sy", "VD")
            for t in range(4):
                p.dma("sy", mstout[:, 4 * t:4 * (t + 1), :], B1[:, 4 * t:4 * (t + 1), :], "ST")
            p.wait("sy", "ST")
        p.barrier()
        for e in ENGS:
            p.wait(e, "ST", p.n["ST"])

        lp = nc.allow_low_precision(reason="fp16 residual stream / bf16 operands by design; PSUM accumulation is fp32")
        lp.__enter__()
        with nc.Block() as block:
            @block.tensor
            def _(e): p.replay(e, "te")
            @block.vector
            def _(e): p.replay(e, "ve")
            @block.scalar
            def _(e): p.replay(e, "se")
            @block.gpsimd
            def _(e): p.replay(e, "ge")
            @block.sync
            def _(e): p.replay(e, "sy")
        lp.__exit__(None, None, None)
    return nc


def kernel(**inputs):
    g = host_prep(inputs)
    x = np.asarray(inputs["L0_attn_x"], np.float32)
    nc = build_program()
    in_maps = []
    for b in range(8):
        m = {"d_" + k: v for k, v in g.items()}
        m["d_x"] = prep_x(x[b])
        in_maps.append(m)
    res = run_bass_kernel_spmd(nc, in_maps, core_ids=list(range(8)))
    out = np.stack([unprep_out(r["out"]) for r in res.results])
    return out.astype(np.float32)
